# version 09
# baseline (speedup 1.0000x reference)
import math
import numpy as np
import ml_dtypes
import concourse.bass as bass
import concourse.mybir as mybir
from concourse.bass_utils import run_bass_kernel_spmd

F32 = mybir.dt.float32
BF16 = mybir.dt.bfloat16
AF = mybir.ActivationFunctionType
ALU = mybir.AluOpType
AX = mybir.AxisListType.X

D = 4096
DC = 32
P = 128
TOWN = 2048
NKV = 8
HD = 128
SLOT_CNT = (8, 16, 24, 32)
OWNBLK = ((0, 3, 4, 7), (1, 2, 5, 6))
INV_SQRT_HD = 1.0 / math.sqrt(HD)
EPS = 1e-5

ENGINES = ("tensor", "vector", "scalar", "gpsimd", "sync")


class Plan:

    def __init__(self):
        self.ops = {e: [] for e in ENGINES}
        self.cnt = {"t": 0, "v": 0, "a": 0, "p": 0, "dout": 0}

    def ensure(self, name):
        if name not in self.cnt:
            self.cnt[name] = 0

    def add(self, eng, fn, waits=(), inc=None, inc_by=1):
        for s, val in waits:
            self.ensure(s)
            assert val <= self.cnt[s], f"wait {s}>={val} planned only {self.cnt[s]}"
        tgt = None
        if inc is not None:
            self.ensure(inc)
            self.cnt[inc] += inc_by
            tgt = (inc, self.cnt[inc])
        self.ops[eng].append((fn, tuple((s, v) for s, v in waits if v > 0), tgt))
        return tgt

    def snap(self):
        return dict(self.cnt)

    def barrier(self):
        c = self.snap()
        for e in ENGINES:
            self.ops[e].append((None, tuple((s, c[s]) for s in c if c[s] > 0), None))


def build_nc(dbg=False):
    nc = bass.Bass("TRN2", target_bir_lowering=False, debug=False)
    dbg_outs = {}
    def dbg_dram(nm, shp, dt):
        ap = nc.dram_tensor(nm, shp, dt, kind="ExternalOutput").ap()
        dbg_outs[nm] = ap
        return ap

    xT = nc.dram_tensor("xT", [D, 4096], F32, kind="ExternalInput").ap()
    xown = nc.dram_tensor("xown", [D, TOWN], F32, kind="ExternalInput").ap()
    wq = nc.dram_tensor("wq", [D, D], F32, kind="ExternalInput").ap()
    wk = nc.dram_tensor("wk", [D, 1024], F32, kind="ExternalInput").ap()
    wv = nc.dram_tensor("wv", [D, 1024], F32, kind="ExternalInput").ap()
    wo = nc.dram_tensor("wo", [D, D], F32, kind="ExternalInput").ap()
    wfT = nc.dram_tensor("wfT", [D, D], F32, kind="ExternalInput").ap()
    vecs = nc.dram_tensor("vecs", [P, 6 * DC], F32, kind="ExternalInput").ap()
    cosb = nc.dram_tensor("cosb", [P, 4096], BF16, kind="ExternalInput").ap()
    sinx = nc.dram_tensor("sinx", [P, 4096], BF16, kind="ExternalInput").ap()
    cosq = nc.dram_tensor("cosq", [P, TOWN], BF16, kind="ExternalInput").ap()
    sinq = nc.dram_tensor("sinq", [P, TOWN], BF16, kind="ExternalInput").ap()
    masks = nc.dram_tensor("masks", [32 * P, 512], BF16, kind="ExternalInput").ap()
    outf = nc.dram_tensor("outf", [D, TOWN], F32, kind="ExternalOutput").ap()

    xn_sc = nc.dram_tensor("xn_sc", [D, 4096], BF16, kind="Internal").ap()
    k_sc = nc.dram_tensor("k_sc", [NKV * P, 4096], BF16, kind="Internal").ap()
    v_sc = nc.dram_tensor("v_sc", [4096, NKV * HD], BF16, kind="Internal").ap()
    q_sc = nc.dram_tensor("q_sc", [32 * P, TOWN], BF16, kind="Internal").ap()
    y_sc = nc.dram_tensor("y_sc", [D, TOWN], BF16, kind="Internal").ap()
    ffn_sc = nc.dram_tensor("ffn_sc", [D, TOWN], F32, kind="Internal").ap()
    attn_sc = nc.dram_tensor("attn_sc", [32 * P, TOWN], BF16, kind="Internal").ap()

    xT3 = xT.rearrange("(c p) t -> p c t", p=P)
    xown3 = xown.rearrange("(c p) t -> p c t", p=P)
    xn3 = xn_sc.rearrange("(c p) t -> p c t", p=P)
    wq3 = wq.rearrange("(c p) o -> p c o", p=P)
    wf3 = wfT.rearrange("(c p) o -> p c o", p=P)
    wo3 = wo.rearrange("(c p) o -> p c o", p=P)
    wk3 = wk.rearrange("(c p) o -> p c o", p=P)
    wv3 = wv.rearrange("(c p) o -> p c o", p=P)
    k3 = k_sc.rearrange("(h p) t -> p h t", p=P)
    v3 = v_sc.rearrange("(kb p) c -> p kb c", p=P)
    masks3 = masks.rearrange("(m p) c -> p m c", p=P)

    probe_a = nc.alloc_sbuf_tensor("arena_probe", [P, 32], mybir.dt.uint8)
    base0 = nc.lookup_mloc(probe_a).addr + 32
    ARENA = 229376 - base0 - 2048
    arena = nc.alloc_sbuf_tensor("arena", [P, ARENA], mybir.dt.uint8)
    base = nc.lookup_mloc(arena).addr
    assert base == base0, (base, base0)
    cursor = [base]

    def sb(name, shape, dtype):
        nbytes = shape[1] * (2 if dtype == BF16 else 4)
        at = cursor[0]
        cursor[0] = at + ((nbytes + 31) // 32) * 32
        assert cursor[0] <= base + ARENA, (name, cursor[0])
        return nc.alloc_sbuf_tensor_at(name, shape, dtype, offset=at)

    ones_b = sb("ones_b", [P, 128], BF16)
    ones_f = sb("ones_f", [P, 4], F32)
    vecs_sb = sb("vecs_sb", [P, 6 * DC], F32)
    g_col = sb("g_col", [P, DC], F32)
    h_col = sb("h_col", [P, DC], F32)
    bias_col = sb("bias_col", [P, DC], F32)
    gam_col = sb("gam_col", [P, DC], F32)
    bet_col = sb("bet_col", [P, DC], F32)
    bh_col = sb("bh_col", [P, DC], F32)
    ksqmax = sb("ksqmax", [P, 8], F32)
    qsqmax = sb("qsqmax", [P, 32], F32)
    norms = sb("norms", [P, 256], F32)
    m_bias = sb("m_bias", [P, 32], F32)
    strip = sb("strip", [P, 2048], F32)
    stripb = sb("stripb", [P, 1024], BF16)
    vstage = sb("vstage", [P, 1024], F32)
    PERSIST_END = cursor[0]
    assert PERSIST_END < 44 * 1024, PERSIST_END

    wk_sb = sb("wk_sb", [P, DC * 1024], BF16)
    wv_sb = sb("wv_sb", [P, DC * 1024], BF16)
    A1_DYN = cursor[0]
    xst = [sb(f"xst{i}", [P, D], F32) for i in range(2)]
    xbA = [sb(f"xbA{i}", [P, D], BF16) for i in range(2)]
    xnt = sb("xnt", [P, D], BF16)
    xsqA = xnt
    abc_sb = sb("abc_sb", [P, 128], BF16)
    bbc_sb = sb("bbc_sb", [P, 128], BF16)
    A1A_END = cursor[0]
    cursor[0] = A1_DYN
    cos_sb = sb("cos_sb", [P, 4096], BF16)
    sin_sb = sb("sin_sb", [P, 4096], BF16)
    xng = [sb("xng0", [P, DC * 256], BF16)] * 2
    kf = [sb(f"kf{i}", [P, 256], BF16) for i in range(2)]
    kst = [sb(f"kst{i}", [P, NKV * 256], BF16) for i in range(2)]
    ksqt = [sb(f"ksqt{i}", [P, 256], BF16) for i in range(2)]
    vbb = [sb(f"vbb{i}", [P, 1024], BF16) for i in range(2)]
    t1p = sb("t1p", [P, 256], F32)
    t2p = sb("t2p", [P, 256], F32)
    A1B_END = cursor[0]
    assert max(A1A_END, A1B_END) <= 229376, (A1A_END, A1B_END)

    cursor[0] = PERSIST_END
    xn_own = sb("xn_own", [P, DC * 1024], BF16)
    wstg = [sb(f"wstg{i}", [P, DC * 128], F32) for i in range(2)]
    wbf = [sb(f"wbf{i}", [P, DC * 256], BF16) for i in range(2)]
    cosq_sb = sb("cosq_sb", [P, TOWN], BF16)
    sinq_sb = sb("sinq_sb", [P, TOWN], BF16)
    qf_sb = sb("qf_sb", [P, 1024], BF16)
    qrot = [sb(f"qrot{i}", [P, 1024], BF16) for i in range(2)]
    qsq_sb = sb("qsq_sb", [P, 1024], BF16)
    yb = [sb(f"yb{i}", [P, 1024], BF16) for i in range(2)]
    xstA2 = sb("xstA2", [P, D], F32)
    xbA2 = sb("xbA2", [P, D], BF16)
    xsqA2 = sb("xsqA2", [P, D], BF16)
    t1q = sb("t1q", [P, 1024], F32)
    t2q = sb("t2q", [P, 1024], F32)
    A2_END = cursor[0]
    assert A2_END <= 229376, A2_END

    cursor[0] = PERSIST_END
    ybuf = sb("ybuf", [P, DC * 1024], BF16)
    ysq = [sb(f"ysq{i}", [P, 1024], BF16) for i in range(2)]
    fout = [sb(f"fout{i}", [P, 1024], F32) for i in range(2)]
    tt1 = sb("tt1", [P, 1024], F32)
    C1_END = cursor[0]
    assert C1_END <= 229376, C1_END

    cursor[0] = PERSIST_END
    masks_sb = sb("masks_sb", [P, 32 * 512], BF16)
    kbuf = [sb(f"kbuf{i}", [P, 4096], BF16) for i in range(2)]
    vbuf = [sb(f"vbuf{i}", [P, 4096], BF16) for i in range(2)]
    qbuf = [sb(f"qbuf{i}", [P, TOWN], BF16) for i in range(2)]
    pT = [sb(f"pT{i}", [P, 1024], BF16) for i in range(4)]
    lacc = [sb(f"lacc{i}", [P, 512], F32) for i in range(2)]
    linv_bc = sb("linv_bc", [P, 512], F32)
    aout = [sb(f"aout{i}", [P, 512], BF16) for i in range(2)]
    B_END = cursor[0]
    assert B_END <= 229376, B_END

    cursor[0] = PERSIST_END
    abuf = sb("abuf", [P, DC * 1024], BF16)
    wstgC = [sb(f"wstgC{i}", [P, DC * 128], F32) for i in range(2)]
    wbfC = [sb(f"wbfC{i}", [P, DC * 256], BF16) for i in range(2)]
    xtile = [sb(f"xtile{i}", [P, 1024], F32) for i in range(2)]
    ftile = [sb(f"ftile{i}", [P, 1024], F32) for i in range(2)]
    otmp = sb("otmp", [P, 1024], F32)
    obuf = [sb(f"obuf{i}", [P, 1024], F32) for i in range(2)]
    C2_END = cursor[0]
    assert C2_END <= 229376, C2_END

    psA = nc.alloc_psum_tensor("psA", [P, 2048], F32)
    psB = nc.alloc_psum_tensor("psB", [P, 2048], F32)

    pl = Plan()
    A = pl.add

    def dma(eng, out_ap, in_ap, waits=(), q="dout"):
        def fn(e, out_ap=out_ap, in_ap=in_ap):
            return e.dma_start(out=out_ap, in_=in_ap)
        return A(eng, fn, waits=[w for w in waits if w], inc=q, inc_by=16)

    def dnow():
        return ("dout", pl.cnt.get("dout", 0))

    def bcast_over(handle, ap2, reps):
        return bass.AP(handle, ap2.offset,
                       [list(ap2.ap[0]), [0, reps], list(ap2.ap[1])])

    def chunk_bcast(handle, ap2, inner):
        return bass.AP(handle, ap2.offset,
                       [list(ap2.ap[0]), list(ap2.ap[1]), [0, inner]])

    t_vec = dma("sync", vecs_sb[:], vecs, q="dc")
    A("vector", lambda e: e.memset(ones_b[:], 1.0))
    A("vector", lambda e: e.memset(ones_f[:], 1.0))
    A("vector", lambda e: e.memset(ksqmax[:], 0.0))
    A("vector", lambda e: e.memset(qsqmax[:], 0.0))
    A("vector", lambda e: e.tensor_copy(bias_col[:], vecs_sb[:, 0:DC]), waits=[t_vec])
    A("vector", lambda e: e.tensor_copy(g_col[:], vecs_sb[:, DC:2 * DC]))
    A("vector", lambda e: e.tensor_copy(h_col[:], vecs_sb[:, 2 * DC:3 * DC]))
    A("vector", lambda e: e.tensor_copy(gam_col[:], vecs_sb[:, 3 * DC:4 * DC]))
    A("vector", lambda e: e.tensor_copy(bet_col[:], vecs_sb[:, 4 * DC:5 * DC]))
    A("vector", lambda e: e.tensor_mul(bh_col[:], bias_col[:], h_col[:]), inc="v")

    xb_free = [None, None]
    xnt_free = [None, None]
    stats_free = None
    bca_free = None
    vstage_free = [None]
    wkv_cast_last = None
    xn_written = {}

    wkv_iter = [("wk", dc) for dc in range(DC)] + [("wv", dc) for dc in range(DC)]
    wkv_pos = 0

    def plan_wkv(n):
        nonlocal wkv_pos, wkv_cast_last
        for _ in range(n):
            if wkv_pos >= len(wkv_iter):
                return
            which, dc = wkv_iter[wkv_pos]
            src = (wk3 if which == "wk" else wv3)[:, dc:dc + 1, :]
            dst = vstage[:].unsqueeze(1)
            tgt = dma("sync", dst, src, waits=[vstage_free[0]], q="dw")
            wsb = wk_sb if which == "wk" else wv_sb
            def cast_fn(e, wsb=wsb, dc=dc):
                return e.tensor_copy(wsb[:, dc * 1024:(dc + 1) * 1024], vstage[:])
            c = A("vector", cast_fn, waits=[tgt], inc="v")
            vstage_free[0] = c
            wkv_cast_last = c
            wkv_pos += 1

    def plan_ln_tile(src3, tcols, xst_, xb_, xsq_, stats_ps, bc_ps, pool_dst_fn,
                     war_load, war_cast, war_pool, ldq):
        nonlocal_waits = [w for w in war_load if w]
        t_x = dma("sync", xst_[:].rearrange("p (c t) -> p c t", c=DC),
                  src3[:, :, tcols], waits=nonlocal_waits, q=ldq)
        wcast = [t_x] + [w for w in war_cast if w]
        A("vector", lambda e: e.tensor_copy(xb_[:], xst_[:]), waits=wcast)
        t_xb = A("vector", lambda e: e.tensor_mul(xsq_[:], xb_[:], xb_[:]), inc="v")
        wt = [("v", t_xb[1]), stats_free] if stats_free else [("v", t_xb[1])]
        def stats_fn(e):
            for dc in range(DC):
                e.matmul(stats_ps[0:1, 0:128], ones_b[:, 0:1],
                         xb_[:, dc * 128:(dc + 1) * 128], start=(dc == 0), stop=(dc == DC - 1))
            for dc in range(DC):
                r = e.matmul(stats_ps[0:1, 128:256], ones_b[:, 0:1],
                             xsq_[:, dc * 128:(dc + 1) * 128], start=(dc == 0), stop=(dc == DC - 1))
            return r
        t_st = A("tensor", stats_fn, waits=wt, inc="t")
        def strip_fn(e):
            e.tensor_scalar_mul(strip[0:1, 0:128], stats_ps[0:1, 0:128], 1.0 / D)
            e.tensor_scalar_mul(strip[0:1, 128:256], stats_ps[0:1, 128:256], 1.0 / D)
            e.tensor_mul(strip[0:1, 256:384], strip[0:1, 0:128], strip[0:1, 0:128])
            e.tensor_sub(strip[0:1, 256:384], strip[0:1, 128:256], strip[0:1, 256:384])
            return e.tensor_scalar_add(strip[0:1, 256:384], strip[0:1, 256:384], EPS)
        t_sp = A("vector", strip_fn, waits=[("t", t_st[1])], inc="v")
        def act_fn(e):
            return e.activation(strip[0:1, 640:768], strip[0:1, 256:384], AF.Sqrt)
        t_ac = A("scalar", act_fn, waits=[("v", t_sp[1])], inc="a")
        def strip2_fn(e):
            e.reciprocal(strip[0:1, 384:512], strip[0:1, 640:768])
            e.tensor_mul(strip[0:1, 512:640], strip[0:1, 0:128], strip[0:1, 384:512])
            e.tensor_copy(stripb[0:1, 0:128], strip[0:1, 384:512])
            return e.tensor_copy(stripb[0:1, 128:256], strip[0:1, 512:640])
        t_s2 = A("vector", strip2_fn, waits=[("a", t_ac[1])], inc="v")
        wt2 = [("v", t_s2[1])] + ([bca_free] if bca_free else [])
        def bc_fn(e):
            e.matmul(bc_ps[:, 0:128], ones_b[0:1, :], stripb[0:1, 0:128], start=True, stop=True)
            return e.matmul(bc_ps[:, 128:256], ones_b[0:1, :], stripb[0:1, 128:256], start=True, stop=True)
        t_bc = A("tensor", bc_fn, waits=wt2, inc="t")
        def cpbc_fn(e):
            e.tensor_copy(abc_sb2[:], bc_ps[:, 0:128])
            return e.tensor_copy(bbc_sb2[:], bc_ps[:, 128:256])
        t_cp = A("vector", cpbc_fn, waits=[("t", t_bc[1])], inc="v")
        wp = [("v", t_cp[1])] + [w for w in war_pool if w]
        def pool_fn(e):
            a_b = bcast_over(abc_sb2, abc_sb2[:], DC)
            b_b = bcast_over(bbc_sb2, bbc_sb2[:], DC)
            return pool_dst_fn(e, a_b, b_b)
        t_pl = A("vector", pool_fn, waits=wp, inc="v")
        return t_pl, t_xb, ("v", t_sp[1]), ("v", t_pl[1])

    abc_sb2 = abc_sb
    bbc_sb2 = bbc_sb

    plan_wkv(8)
    sq_prev = None
    xnt_free1 = None
    cast_prev = [None, None]
    for i in range(32):
        b = i % 2
        def mk_pool_dst(b=b):
            def f(e, a_b, b_b):
                src = xbA[b][:].rearrange("p (c t) -> p c t", c=DC)
                dst = xnt[:].rearrange("p (c t) -> p c t", c=DC)
                e.tensor_tensor(dst, src, a_b, ALU.mult)
                return e.tensor_tensor(dst, dst, b_b, ALU.subtract)
            return f
        t_pl, sq_prev, stats_free, bca_free = plan_ln_tile(
            xT3, slice(i * 128, (i + 1) * 128), xst[b], xbA[b], xsqA,
            psA[:, 0:512], psA[:, 512:1024], mk_pool_dst(),
            war_load=[cast_prev[b]], war_cast=[xb_free[b], xnt_free1],
            war_pool=[xnt_free1], ldq="dx")
        cast_prev[b] = sq_prev
        xb_free[b] = bca_free
        t_wr = dma("scalar", xn3[:, :, i * 128:(i + 1) * 128],
                   xnt[:].rearrange("p (c t) -> p c t", c=DC),
                   waits=[bca_free], q="dout")
        xnt_free1 = ("dout", t_wr[1])
        xn_written[i] = t_wr[1]
        if dbg and i == 31:
            d1 = dbg_dram("d_xsq31", [P, D], BF16)
            d2 = dbg_dram("d_strip31", [1, 1024], F32)
            d3 = dbg_dram("d_xb31", [P, D], BF16)
            d4 = dbg_dram("d_abc31", [P, 128], BF16)
            d5 = dbg_dram("d_bbc31", [P, 128], BF16)
            dma("sync", d1, xsqA[:], waits=[stats_free], q="dc")
            dma("sync", d2, strip[0:1, 0:1024], waits=[stats_free, bca_free], q="dc")
            dma("sync", d3, xbA[i % 2][:], waits=[stats_free], q="dc")
            dma("sync", d4, abc_sb[:], waits=[bca_free])
            dma("sync", d5, bbc_sb[:], waits=[bca_free])
        plan_wkv(2)
    plan_wkv(len(wkv_iter) - wkv_pos)

    pl.barrier()

    t_cos = dma("sync", cos_sb[:], cosb, q="dc")
    t_sin = dma("sync", sin_sb[:], sinx, q="dc")
    GTOK = 256
    NG = 4096 // GTOK
    psK_free = [None, None]
    psV_free = [None, None]
    xng_free = [None, None]
    kf_free = [None, None]
    kst_free = [None, None]
    ksqt_free = [None, None]
    vbb_free = [None, None]
    kv_written = []
    psKv = [psA[:, 1024:1280], psA[:, 1536:1792]]
    psKstrip = [psA[0:1, 1024:1280], psA[0:1, 1536:1792]]
    psVv = [psB[:, 0:1024], psB[:, 1024:2048]]

    for g in range(NG):
        gc = slice(g * GTOK, (g + 1) * GTOK)
        bg = g % 2
        w = [("dout", xn_written[(g * GTOK + GTOK) // 128 - 1]), xng_free[0]]
        if g == 0:
            w += [t_sin, wkv_cast_last]
        t_xn = dma("sync", xng[bg][:].rearrange("p (c t) -> p c t", c=DC),
                   xn3[:, :, gc], waits=w, q="dg")
        last_rp = None
        pending_kq = [None]
        def drain_kq():
            if pending_kq[0] is not None:
                pending_kq[0]()
                pending_kq[0] = None
        for kh in range(NKV):
            bk = (g * NKV + kh) % 2
            wt = [t_xn] + ([psK_free[bk]] if psK_free[bk] else [])
            def kmm_fn(e, bk=bk, kh=kh, bg=bg):
                for dc in range(DC):
                    r = e.matmul(psKv[bk],
                                 wk_sb[:, dc * 1024 + kh * 128: dc * 1024 + (kh + 1) * 128],
                                 xng[bg][:, dc * GTOK:(dc + 1) * GTOK],
                                 start=(dc == 0), stop=(dc == DC - 1))
                return r
            t_k = A("tensor", kmm_fn, waits=wt, inc="t")
            drain_kq()
            wv_ = [("t", t_k[1])]
            if kh == 0 and kst_free[bg]:
                wv_.append(dnow())
            def rope_fn(e, bk=bk, kh=kh, bg=bg, gc=gc):
                dst = kst[bg][:, kh * 256:(kh + 1) * 256]
                e.tensor_tensor(t1p[:], psKv[bk], cos_sb[:, gc], ALU.mult)
                e.tensor_tensor(t2p[0:64, :], psKv[bk][64:128, :], sin_sb[0:64, gc], ALU.mult)
                e.tensor_tensor(t2p[64:128, :], psKv[bk][0:64, :], sin_sb[64:128, gc], ALU.mult)
                return e.tensor_tensor(dst, t1p[:], t2p[:], ALU.add)
            t_rp = A("vector", rope_fn, waits=wv_, inc="v")
            wp = [("v", t_rp[1])] + ([ksqt_free[bk]] if ksqt_free[bk] else [])
            def ksq_fn(e, bk=bk, kh=kh, bg=bg):
                dst = kst[bg][:, kh * 256:(kh + 1) * 256]
                return e.tensor_tensor(ksqt[bk][:], dst, dst, ALU.mult)
            t_sq3 = A("gpsimd", ksq_fn, waits=wp, inc="p")
            def deferred(bk=bk, kh=kh, g=g, t_sq3=t_sq3, t_rp=t_rp):
                t_kq = A("tensor", lambda e, bk=bk: e.matmul(psKstrip[bk], ones_b[:, 0:1],
                                                             ksqt[bk][:], start=True, stop=True),
                         waits=[("p", t_sq3[1]), ("v", t_rp[1])], inc="t")
                def kmax_fn(e, bk=bk, kh=kh, g=g):
                    return e.reduce_max(norms[0:1, kh * 16 + g: kh * 16 + g + 1],
                                        psKstrip[bk], axis=AX)
                t_km = A("vector", kmax_fn, waits=[("t", t_kq[1])], inc="v")
                psK_free[bk] = ("v", t_km[1])
                ksqt_free[bk] = ("v", t_km[1])
            pending_kq[0] = deferred
            last_rp = t_rp
        if dbg and g == 0:
            d6 = dbg_dram("d_wksb", [P, DC * 1024], BF16)
            dma("sync", d6, wk_sb[:], waits=[wkv_cast_last], q="dc")
            d7 = dbg_dram("d_kst0", [P, NKV * 256], BF16)
            dma("sync", d7, kst[bg][:], waits=[("v", last_rp[1])], q="dc")
        drain_kq()
        t_ks = dma("scalar", k3[:, :, gc], kst[bg][:].rearrange("p (h t) -> p h t", h=NKV),
                   waits=[("v", last_rp[1])], q="dout")
        kst_free[bg] = ("dout", t_ks[1])
        kv_written.append(t_ks[1])
        for th in range(2):
            bvp = (g * 2 + th) % 2
            wt = [t_xn] + ([psV_free[bvp]] if psV_free[bvp] else [])
            def vmm_fn(e, bvp=bvp, th=th, bg=bg):
                for dc in range(DC):
                    e.matmul(psVv[bvp][:, 0:512],
                             xng[bg][:, dc * GTOK + th * 128: dc * GTOK + th * 128 + 128],
                             wv_sb[:, dc * 1024: dc * 1024 + 512],
                             start=(dc == 0), stop=(dc == DC - 1))
                for dc in range(DC):
                    r = e.matmul(psVv[bvp][:, 512:1024],
                                 xng[bg][:, dc * GTOK + th * 128: dc * GTOK + th * 128 + 128],
                                 wv_sb[:, dc * 1024 + 512: dc * 1024 + 1024],
                                 start=(dc == 0), stop=(dc == DC - 1))
                return r
            t_v = A("tensor", vmm_fn, waits=wt, inc="t")
            wv_ = [("t", t_v[1])] + ([dnow()] if vbb_free[bvp] else [])
            t_vc = A("vector", lambda e, bvp=bvp: e.tensor_copy(vbb[bvp][:], psVv[bvp]),
                     waits=wv_, inc="v")
            psV_free[bvp] = ("v", t_vc[1])
            t_vs = dma("scalar", v_sc[(g * GTOK + th * 128):(g * GTOK + th * 128 + 128), :],
                       vbb[bvp][:], waits=[("v", t_vc[1])], q="dout")
            vbb_free[bvp] = ("dout", t_vs[1])
            kv_written.append(t_vs[1])
        xng_free[0] = ("t", pl.cnt["t"])

    pl.barrier()

    t_cq = dma("sync", cosq_sb[:], cosq, q="dc")
    t_sq = dma("sync", sinq_sb[:], sinq, q="dc")
    xbA2_free = None
    wstg_free = [None, None]
    wbf_free = [None, None]
    psQ_free = [None, None, None]
    qsq_free = None
    qidx = [0]
    qrot_free = [None, None]
    qrot_free2 = [None, None]
    yb_free = [None, None]
    q_written = {}
    y_written = {}
    psQv = [psA[:, 0:1024], psA[:, 1024:2048], psB[:, 1024:2048]]
    abc_sb2 = abc_sb
    bbc_sb2 = bbc_sb
    stats_free = None
    bca_free = None

    sqA2_prev = None
    for th in range(2):
        xn_ready = None
        xn_half = [None, None]
        for j in range(8):
            def mk_pool_dst(j=j):
                def f(e, a_b, b_b):
                    dst = bass.AP(xn_own, xn_own[:].offset + j * 128,
                                  [list(xn_own[:].ap[0]), [1024, DC], [1, 128]])
                    src = xbA2[:].rearrange("p (c t) -> p c t", c=DC)
                    e.tensor_tensor(dst, src, a_b, ALU.mult)
                    return e.tensor_tensor(dst, dst, b_b, ALU.subtract)
                return f
            war_pool = [xbA2_free]
            if th == 1 and j == 0:
                war_pool.append(("t", pl.cnt["t"]))
            t_pl, sqA2_prev, stats_free, bca_free = plan_ln_tile(
                xown3, slice(th * 1024 + j * 128, th * 1024 + (j + 1) * 128),
                xstA2, xbA2, xsqA2,
                psB[:, 0:512], psB[:, 512:1024], mk_pool_dst(),
                war_load=[sqA2_prev], war_cast=[xbA2_free], war_pool=war_pool, ldq="dx2")
            xbA2_free = bca_free
            xn_ready = bca_free
            if j == 3:
                xn_half[0] = bca_free
            if j == 7:
                xn_half[1] = bca_free
        pending_qn = [None]
        def drain_qn():
            if pending_qn[0] is not None:
                pending_qn[0]()
                pending_qn[0] = None
        for w2 in range(32):
            is_q = w2 < 16
            duo = w2 if is_q else w2 - 16
            bsel = w2 % 2
            cast_tgts = [None, None]
            for half in range(2):
                src3 = (wq3 if is_q else wf3)[:, :, duo * 256 + half * 128: duo * 256 + (half + 1) * 128]
                t_w = dma("sync", wstg[half][:].rearrange("p (c f) -> p c f", c=DC), src3,
                          waits=[wstg_free[half]], q="dw2")
                wc = [t_w] + ([wbf_free[bsel]] if (wbf_free[bsel] and half == 0) else [])
                dst3 = bass.AP(wbf[bsel], wbf[bsel][:].offset + half * 128,
                               [list(wbf[bsel][:].ap[0]), [256, DC], [1, 128]])
                if is_q:
                    def cast_fn(e, bsel=bsel, half=half, dst3=dst3):
                        return e.tensor_copy(dst3, wstg[half][:].rearrange("p (c f) -> p c f", c=DC))
                    t_cast = A("vector", cast_fn, waits=wc, inc="v")
                    wstg_free[half] = t_cast
                else:
                    def sgn_fn(e, bsel=bsel, half=half, dst3=dst3):
                        return e.activation(dst3, wstg[half][:].rearrange("p (c f) -> p c f", c=DC),
                                            AF.Sign)
                    t_sgn = A("scalar", sgn_fn, waits=wc, inc="a")
                    wstg_free[half] = t_sgn
                    def gmul_fn(e, bsel=bsel, dst3=dst3):
                        g_b = chunk_bcast(g_col, g_col[:], 128)
                        return e.tensor_tensor(dst3, dst3, g_b, ALU.mult)
                    t_cast = A("vector", gmul_fn, waits=[t_sgn], inc="v")
                cast_tgts[half] = t_cast
            for sub in range(2):
                oc = duo * 2 + sub
                bq = qidx[0] % 3
                b2 = qidx[0] % 2
                qidx[0] += 1
                t_cast = cast_tgts[sub]
                wt = [("v", t_cast[1]), xn_half[0]] + ([psQ_free[bq]] if psQ_free[bq] else [])
                def wmm_fn(e, bsel=bsel, sub=sub, bq=bq):
                    for dc in range(DC):
                        r = e.matmul(psQv[bq][:, 0:512],
                                     wbf[bsel][:, dc * 256 + sub * 128: dc * 256 + sub * 128 + 128],
                                     xn_own[:, dc * 1024: dc * 1024 + 512],
                                     start=(dc == 0), stop=(dc == DC - 1))
                    return r
                t_mma = A("tensor", wmm_fn, waits=wt, inc="t")
                def wmm_fn2(e, bsel=bsel, sub=sub, bq=bq):
                    for dc in range(DC):
                        r = e.matmul(psQv[bq][:, 512:1024],
                                     wbf[bsel][:, dc * 256 + sub * 128: dc * 256 + sub * 128 + 128],
                                     xn_own[:, dc * 1024 + 512: dc * 1024 + 1024],
                                     start=(dc == 0), stop=(dc == DC - 1))
                    return r
                t_mm = A("tensor", wmm_fn2, waits=[xn_half[1]], inc="t")
                drain_qn()
                if sub == 1:
                    wbf_free[bsel] = ("t", t_mm[1])
                if is_q:
                    h = oc
                    wv_ = [("t", t_mm[1]), t_cq, t_sq]
                    if qrot_free[b2]:
                        wv_.append(dnow())
                    if qrot_free2[b2]:
                        wv_.append(qrot_free2[b2])
                    def qrope_fn(e, bq=bq, b2=b2, th=th):
                        cs = slice(th * 1024, (th + 1) * 1024)
                        e.tensor_tensor(t1q[:], psQv[bq], cosq_sb[:, cs], ALU.mult)
                        e.tensor_tensor(t2q[0:64, :], psQv[bq][64:128, :], sinq_sb[0:64, cs], ALU.mult)
                        e.tensor_tensor(t2q[64:128, :], psQv[bq][0:64, :], sinq_sb[64:128, cs], ALU.mult)
                        return e.tensor_tensor(qrot[b2][:], t1q[:], t2q[:], ALU.add)
                    t_rv = A("vector", qrope_fn, waits=wv_, inc="v")
                    wp = [("v", t_rv[1])] + ([qsq_free] if qsq_free else [])
                    def qsq_fn(e, b2=b2):
                        return e.tensor_tensor(qsq_sb[:], qrot[b2][:], qrot[b2][:], ALU.mult)
                    t_rp = A("gpsimd", qsq_fn, waits=wp, inc="p")
                    def deferred_qn(bq=bq, h=h, th=th, t_rp=t_rp, t_rv=t_rv):
                        def qn_fn(e, bq=bq):
                            e.matmul(psQv[bq][0:1, 0:512], ones_b[:, 0:1], qsq_sb[:, 0:512],
                                     start=True, stop=True)
                            return e.matmul(psQv[bq][0:1, 512:1024], ones_b[:, 0:1],
                                            qsq_sb[:, 512:1024], start=True, stop=True)
                        t_qn = A("tensor", qn_fn, waits=[("p", t_rp[1]), ("v", t_rv[1])], inc="t")
                        def qmax_fn(e, bq=bq, h=h, th=th):
                            e.tensor_copy(strip[0:1, 1024:1536], psQv[bq][0:1, 0:512])
                            e.tensor_copy(strip[0:1, 1536:2048], psQv[bq][0:1, 512:1024])
                            return e.reduce_max(norms[0:1, 128 + h * 2 + th: 129 + h * 2 + th],
                                                strip[0:1, 1024:2048], axis=AX)
                        t_qm = A("vector", qmax_fn, waits=[("t", t_qn[1])], inc="v")
                        psQ_free[bq] = ("v", t_qm[1])
                        globals()
                        nonlocal qsq_free
                        qsq_free = ("t", t_qn[1])
                    pending_qn[0] = deferred_qn
                    t_qs = dma("scalar", q_sc[h * P:(h + 1) * P, th * 1024:(th + 1) * 1024],
                               qrot[b2][:], waits=[("v", t_rv[1])], q="dout")
                    qrot_free[b2] = ("dout", t_qs[1])
                    qrot_free2[b2] = ("p", t_rp[1])
                    q_written[(h, th)] = t_qs[1]
                else:
                    o = oc
                    wv_ = [("t", t_mm[1])] + ([dnow()] if yb_free[b2] else [])
                    def yevac_fn(e, bq=bq, b2=b2, o=o):
                        return e.tensor_scalar(yb[b2][:], psQv[bq], h_col[:, o:o + 1],
                                               bh_col[:, o:o + 1], ALU.mult, ALU.add)
                    t_ye = A("vector", yevac_fn, waits=wv_, inc="v")
                    psQ_free[bq] = ("v", t_ye[1])
                    t_ys = dma("scalar", y_sc[o * P:(o + 1) * P, th * 1024:(th + 1) * 1024],
                               yb[b2][:], waits=[("v", t_ye[1])], q="dout")
                    yb_free[b2] = ("dout", t_ys[1])
                    y_written[(o, th)] = t_ys[1]

        drain_qn()

    pl.barrier()

    ysq_free = [None, None]
    fout_free = [None, None]
    ffn_written = {}
    for th in range(2):
        ydin = []
        for oc in range(DC):
            w = [dnow()] if oc == 0 else []
            if th == 1 and oc == 0:
                w.append(("v", pl.cnt["v"]))
            t_y = dma("sync", ybuf[:, oc * 1024:(oc + 1) * 1024],
                      y_sc[oc * P:(oc + 1) * P, th * 1024:(th + 1) * 1024], waits=w,
                      q="dy1")
            ydin.append(t_y)
        last_sq = None
        for oc in range(DC):
            wv_ = [ydin[-1]] + ([ysq_free[oc % 2]] if ysq_free[oc % 2] else [])
            t_sq2 = A("vector", lambda e, oc=oc: e.tensor_mul(
                ysq[oc % 2][:], ybuf[:, oc * 1024:(oc + 1) * 1024],
                ybuf[:, oc * 1024:(oc + 1) * 1024]), waits=wv_, inc="v")
            def ymm_fn(e, oc=oc):
                st, sp = oc == 0, oc == DC - 1
                e.matmul(psA[0:1, 0:512], ones_b[:, 0:1],
                         ybuf[:, oc * 1024: oc * 1024 + 512], start=st, stop=sp)
                e.matmul(psA[0:1, 512:1024], ones_b[:, 0:1],
                         ybuf[:, oc * 1024 + 512: oc * 1024 + 1024], start=st, stop=sp)
                e.matmul(psA[0:1, 1024:1536], ones_b[:, 0:1],
                         ysq[oc % 2][:, 0:512], start=st, stop=sp)
                return e.matmul(psA[0:1, 1536:2048], ones_b[:, 0:1],
                                ysq[oc % 2][:, 512:1024], start=st, stop=sp)
            t_my = A("tensor", ymm_fn, waits=[("v", t_sq2[1])], inc="t")
            ysq_free[oc % 2] = ("t", t_my[1])
            last_sq = t_my
        def stripC_fn(e):
            e.tensor_scalar_mul(strip[0:1, 0:1024], psA[0:1, 0:1024], 1.0 / D)
            e.tensor_scalar_mul(strip[0:1, 1024:2048], psA[0:1, 1024:2048], 1.0 / D)
            e.tensor_mul(tt1[0:1, 0:1024], strip[0:1, 0:1024], strip[0:1, 0:1024])
            e.tensor_sub(strip[0:1, 1024:2048], strip[0:1, 1024:2048], tt1[0:1, 0:1024])
            return e.tensor_scalar_add(strip[0:1, 1024:2048], strip[0:1, 1024:2048], EPS)
        t_sp = A("vector", stripC_fn, waits=[("t", last_sq[1])], inc="v")
        def actC_fn(e):
            return e.activation(tt1[0:1, 0:1024], strip[0:1, 1024:2048], AF.Sqrt)
        t_ac = A("scalar", actC_fn, waits=[("v", t_sp[1])], inc="a")
        def strip2C_fn(e):
            e.reciprocal(strip[0:1, 1024:2048], tt1[0:1, 0:1024])
            e.tensor_mul(strip[0:1, 0:1024], strip[0:1, 0:1024], strip[0:1, 1024:2048])
            e.tensor_copy(stripb[0:1, 0:1024], strip[0:1, 1024:2048])
            return e.tensor_copy(ysq[0][0:1, 0:1024], strip[0:1, 0:1024])
        t_s2 = A("vector", strip2C_fn, waits=[("a", t_ac[1])], inc="v")
        def bcC_fn(e):
            e.matmul(psB[:, 0:512], ones_b[0:1, :], stripb[0:1, 0:512], start=True, stop=True)
            e.matmul(psB[:, 512:1024], ones_b[0:1, :], stripb[0:1, 512:1024], start=True, stop=True)
            e.matmul(psB[:, 1024:1536], ones_b[0:1, :], ysq[0][0:1, 0:512], start=True, stop=True)
            return e.matmul(psB[:, 1536:2048], ones_b[0:1, :], ysq[0][0:1, 512:1024], start=True, stop=True)
        t_bc = A("tensor", bcC_fn, waits=[("v", t_s2[1])], inc="t")
        ysq_free[0] = ("t", t_bc[1])
        for oc in range(DC):
            bf = oc % 2
            wv_ = [("t", t_bc[1])] + ([dnow()] if fout_free[bf] else [])
            def norm_fn(e, oc=oc, bf=bf):
                e.tensor_tensor(tt1[:], ybuf[:, oc * 1024:(oc + 1) * 1024], psB[:, 0:1024], ALU.mult)
                e.tensor_tensor(tt1[:], tt1[:], psB[:, 1024:2048], ALU.subtract)
                return e.tensor_scalar(fout[bf][:], tt1[:], gam_col[:, oc:oc + 1],
                                       bet_col[:, oc:oc + 1], ALU.mult, ALU.add)
            t_nm = A("vector", norm_fn, waits=wv_, inc="v")
            t_fs = dma("scalar", ffn_sc[oc * P:(oc + 1) * P, th * 1024:(th + 1) * 1024],
                       fout[bf][:], waits=[("v", t_nm[1])], q="dout")
            fout_free[bf] = ("dout", t_fs[1])
            ffn_written[(oc, th)] = t_fs[1]

    pl.barrier()

    b_start_dout = dnow()
    t_msk = dma("sync", masks_sb[:].rearrange("p (m c) -> p m c", m=32), masks3, q="dc")
    def mprep_fn(e):
        for kh in range(8):
            e.reduce_max(ksqmax[0:1, kh:kh + 1], norms[0:1, kh * 16:(kh + 1) * 16], axis=AX)
        ap0 = bass.AP(norms, norms[:].offset + 128, [list(norms[:].ap[0]), [2, 32]])
        ap1 = bass.AP(norms, norms[:].offset + 129, [list(norms[:].ap[0]), [2, 32]])
        return e.tensor_tensor(qsqmax[0:1, 0:32], ap0[0:1], ap1[0:1], ALU.max)
    t_mp = A("vector", mprep_fn, waits=[], inc="v")
    def mmul_fn(e):
        r = None
        for kh in range(8):
            kb = bass.AP(ksqmax, ksqmax[:].offset + kh, [list(ksqmax[:].ap[0]), [0, 4]])
            r = e.tensor_tensor(strip[0:1, kh * 4:(kh + 1) * 4],
                                qsqmax[0:1, kh * 4:(kh + 1) * 4], kb[0:1], ALU.mult)
        return r
    t_mm2 = A("gpsimd", mmul_fn, waits=[("v", t_mp[1])], inc="p")
    def mact_fn(e):
        return e.activation(strip[0:1, 32:64], strip[0:1, 0:32], AF.Sqrt, scale=1.0 / HD)
    t_ma = A("scalar", mact_fn, waits=[("p", t_mm2[1])], inc="a")
    t_mn = A("vector", lambda e: e.tensor_copy(stripb[0:1, 0:32], strip[0:1, 32:64]),
             waits=[("a", t_ma[1])], inc="v")
    t_mb = A("tensor", lambda e: e.matmul(psB[:, 0:32], ones_b[0:1, :], stripb[0:1, 0:32],
                                          start=True, stop=True),
             waits=[("v", t_mn[1])], inc="t")
    t_mc = A("vector", lambda e: e.tensor_scalar_mul(m_bias[:], psB[:, 0:32], -1.0),
             waits=[("t", t_mb[1])], inc="v")

    kbuf_free = [None, None]
    vbuf_free = [None, None]
    qbuf_free = [None, None]
    pT_free = [None, None, None, None]
    lacc_free = [None, None]
    aout_free = [None, None]
    psS_free = [None, None]
    psPV_free = [None, None]
    lsum_free = None
    lbc_free = ("v", t_mc[1])

    psSv = [psA[:, 0:1024], psA[:, 1024:2048]]
    psPVv = [psB[:, 0:512], psB[:, 512:1024]]
    psLSv = [psB[0:1, 1024:1536], psB[0:1, 1536:2048]]
    psLBv = [psB[:, 1024:1536], psB[:, 1536:2048]]
    lsum_free2 = [None, None]
    attn_written = {}
    kv_done = b_start_dout
    gj = 0
    hs = 0

    for kh in range(NKV):
        bk = kh % 2
        wk_ = [kv_done] + ([kbuf_free[bk]] if kbuf_free[bk] else [])
        t_kl = dma("sync", kbuf[bk][:], k_sc[kh * P:(kh + 1) * P, :], waits=wk_,
                   q="dkv%d" % bk)
        wv_ = [kv_done] + ([vbuf_free[bk]] if vbuf_free[bk] else [])
        t_vl = dma("sync", vbuf[bk][:].rearrange("p (kb c) -> p kb c", kb=32),
                   v3[:, :, kh * HD:(kh + 1) * HD], waits=wv_, q="dkv%d" % bk)
        for hh in range(4):
            h = kh * 4 + hh
            bq = h % 2
            wq_ = [b_start_dout]
            if qbuf_free[bq]:
                wq_.append(qbuf_free[bq])
            t_ql = dma("sync", qbuf[bq][:], q_sc[h * P:(h + 1) * P, :], waits=wq_,
                       q="dq%d" % bq)
            for s in range(4):
                cnt = SLOT_CNT[s]
                hs_cur = hs
                pv_slot = psPVv[hs_cur % 2]
                G = cnt // 2
                sc_tgt = [None] * G
                sc_slot = [None] * G

                def plan_scores(j, bk=bk, bq=bq, s=s):
                    bs = (gj0 + j) % 2
                    kb0, kb1 = 2 * j, 2 * j + 1
                    wt = [t_kl, t_vl, t_ql]
                    if psS_free[bs]:
                        wt.append(psS_free[bs])
                    if hs == 0 and j == 0:
                        wt.append(("v", t_mc[1]))
                    def smm_fn(e, bs=bs, kb0=kb0, kb1=kb1, bk=bk, bq=bq, s=s):
                        qv_ = qbuf[bq][:, s * 512:(s + 1) * 512]
                        e.matmul(psSv[bs][:, 0:512], kbuf[bk][:, kb0 * 128:(kb0 + 1) * 128],
                                 qv_, start=True, stop=True)
                        return e.matmul(psSv[bs][:, 512:1024], kbuf[bk][:, kb1 * 128:(kb1 + 1) * 128],
                                        qv_, start=True, stop=True)
                    sc_tgt[j] = A("tensor", smm_fn, waits=wt, inc="t")
                    sc_slot[j] = bs

                gj0 = gj
                plan_scores(0)
                t_pv_last = None
                t_ls_last = None
                pending_pv = [None]
                def drain_pv():
                    if pending_pv[0] is not None:
                        pending_pv[0]()
                        pending_pv[0] = None
                for j in range(G):
                    bs = sc_slot[j]
                    bp = (gj0 + j) % 4
                    kb0, kb1 = 2 * j, 2 * j + 1
                    if j + 1 < G:
                        plan_scores(j + 1)
                    drain_pv()
                    we = [("t", sc_tgt[j][1])] + ([pT_free[bp]] if pT_free[bp] else [])
                    def exp_fn(e, bp=bp, bs=bs, h=h):
                        return e.activation(pT[bp][:], psSv[bs], AF.Exp,
                                            bias=m_bias[:, h:h + 1], scale=INV_SQRT_HD)
                    t_ex = A("scalar", exp_fn, waits=we, inc="a")
                    psS_free[bs] = ("a", t_ex[1])
                    dep = ("a", t_ex[1])
                    if 2 * j >= cnt - 8:
                        w0 = kb0 - (cnt - 8)
                        w1 = kb1 - (cnt - 8)
                        wm = [dep, t_msk]
                        def msk_fn(e, bp=bp, s=s, w0=w0, w1=w1):
                            m0 = masks_sb[:, (s * 8 + w0) * 512:(s * 8 + w0 + 1) * 512]
                            m1 = masks_sb[:, (s * 8 + w1) * 512:(s * 8 + w1 + 1) * 512]
                            e.tensor_tensor(pT[bp][:, 0:512], pT[bp][:, 0:512], m0, ALU.mult)
                            return e.tensor_tensor(pT[bp][:, 512:1024], pT[bp][:, 512:1024], m1, ALU.mult)
                        t_mk = A("vector", msk_fn, waits=wm, inc="v")
                        dep = ("v", t_mk[1])
                    def mk_pvls(dep=dep, bp=bp, kb0=kb0, kb1=kb1, j=j, cnt=cnt, bk=bk):
                        def emit_pvls():
                            nonlocal t_pv_last, t_ls_last
                            wpv = [dep]
                            if j == 0 and psPV_free[hs_cur % 2]:
                                wpv.append(psPV_free[hs_cur % 2])
                            def pv_fn(e, bp=bp, kb0=kb0, kb1=kb1, bk=bk, slot=pv_slot, j=j, cnt=cnt):
                                e.matmul(slot, vbuf[bk][:, kb0 * 128:(kb0 + 1) * 128],
                                         pT[bp][:, 0:512], start=(j == 0), stop=False)
                                return e.matmul(slot, vbuf[bk][:, kb1 * 128:(kb1 + 1) * 128],
                                                pT[bp][:, 512:1024], start=False,
                                                stop=(2 * j + 1 == cnt - 1))
                            t_pv_last = A("tensor", pv_fn, waits=wpv, inc="t")
                            wls = [dep]
                            if j == 0 and lsum_free2[hs_cur % 2]:
                                wls.append(lsum_free2[hs_cur % 2])
                            def ls_fn(e, bp=bp, j=j, cnt=cnt, pls=psLSv[hs_cur % 2]):
                                e.matmul(pls, ones_b[:, 0:1], pT[bp][:, 0:512],
                                         start=(j == 0), stop=False)
                                return e.matmul(pls, ones_b[:, 0:1], pT[bp][:, 512:1024],
                                                start=False, stop=(2 * j + 1 == cnt - 1))
                            t_ls_last = A("tensor", ls_fn, waits=wls, inc="t")
                            pT_free[bp] = ("t", t_ls_last[1])
                        return emit_pvls
                    pending_pv[0] = mk_pvls()
                    gj += 1
                drain_pv()
                def rec_fn(e, pls=psLSv[hs_cur % 2]):
                    e.reciprocal(strip[0:1, 1024:1536], pls)
                    return e.tensor_copy(stripb[0:1, 512:1024], strip[0:1, 1024:1536])
                t_rc = A("vector", rec_fn, waits=[("t", t_ls_last[1])], inc="v")
                wlb = [("v", t_rc[1])] + ([lbc_free] if lbc_free else [])
                t_lb = A("tensor", lambda e, plb=psLBv[hs_cur % 2]: e.matmul(
                             plb[:, 0:512], ones_b[0:1, :],
                             stripb[0:1, 512:1024], start=True, stop=True),
                         waits=wlb, inc="t")
                ba = hs_cur % 2
                wsc = [("t", t_lb[1]), ("t", t_pv_last[1])]
                if aout_free[ba]:
                    wsc.append(dnow())
                def scale_fn(e, ba=ba, slot=pv_slot, plb=psLBv[hs_cur % 2]):
                    e.tensor_copy(linv_bc[:], plb[:, 0:512])
                    return e.tensor_tensor(aout[ba][:], slot, linv_bc[:], ALU.mult)
                t_sc2 = A("vector", scale_fn, waits=wsc, inc="v")
                lbc_free = ("v", t_sc2[1])
                lsum_free2[hs_cur % 2] = ("v", t_sc2[1])
                psPV_free[hs_cur % 2] = ("v", t_sc2[1])
                t_as = dma("scalar", attn_sc[h * P:(h + 1) * P, s * 512:(s + 1) * 512],
                           aout[ba][:], waits=[("v", t_sc2[1])], q="dout")
                aout_free[ba] = ("dout", t_as[1])
                attn_written[(h, s)] = t_as[1]
                hs += 1
            qbuf_free[bq] = ("t", pl.cnt["t"])
        kbuf_free[bk] = ("t", pl.cnt["t"])
        vbuf_free[bk] = ("t", pl.cnt["t"])

    pl.barrier()

    c2_start_dout = dnow()
    wstgC_free = [None, None]
    wbfC_free = [None, None]
    psO_free = [None, None, None, None]
    xt_free = [None, None]
    ft_free = [None, None]
    obuf_free = [None, None]
    psOv = [psA[:, 0:1024], psA[:, 1024:2048], psB[:, 0:1024], psB[:, 1024:2048]]
    oidx = [0]
    for th in range(2):
        adin = []
        for dc in range(DC):
            w = []
            if th == 0 and dc == 0:
                w.append(c2_start_dout)
            if th == 1 and dc == 0:
                w.append(("t", pl.cnt["t"]))
            t_a = dma("sync", abuf[:, dc * 1024:(dc + 1) * 1024],
                      attn_sc[dc * P:(dc + 1) * P, th * 1024:(th + 1) * 1024], waits=w,
                      q="dab")
            adin.append(t_a)
        for w2 in range(16):
            bsel = w2 % 2
            cast_tgts = [None, None]
            for half in range(2):
                t_w = dma("sync", wstgC[half][:].rearrange("p (c f) -> p c f", c=DC),
                          wo3[:, :, w2 * 256 + half * 128: w2 * 256 + (half + 1) * 128],
                          waits=[wstgC_free[half]], q="dw3")
                wc = [t_w] + ([wbfC_free[bsel]] if (wbfC_free[bsel] and half == 0) else [])
                dst3 = bass.AP(wbfC[bsel], wbfC[bsel][:].offset + half * 128,
                               [list(wbfC[bsel][:].ap[0]), [256, DC], [1, 128]])
                def castC_fn(e, half=half, dst3=dst3):
                    return e.tensor_copy(dst3, wstgC[half][:].rearrange("p (c f) -> p c f", c=DC))
                t_cast = A("vector", castC_fn, waits=wc, inc="v")
                wstgC_free[half] = t_cast
                cast_tgts[half] = t_cast
            for sub in range(2):
                oc = w2 * 2 + sub
                bo = oidx[0] % 4
                b2o = oidx[0] % 2
                oidx[0] += 1
                t_cast = cast_tgts[sub]
                wt = [("v", t_cast[1]), adin[-1]]
                if psO_free[bo]:
                    wt.append(psO_free[bo])
                def omm_fn(e, bsel=bsel, sub=sub, bo=bo):
                    for dc in range(DC):
                        e.matmul(psOv[bo][:, 0:512],
                                 wbfC[bsel][:, dc * 256 + sub * 128: dc * 256 + sub * 128 + 128],
                                 abuf[:, dc * 1024: dc * 1024 + 512],
                                 start=(dc == 0), stop=(dc == DC - 1))
                    for dc in range(DC):
                        r = e.matmul(psOv[bo][:, 512:1024],
                                     wbfC[bsel][:, dc * 256 + sub * 128: dc * 256 + sub * 128 + 128],
                                     abuf[:, dc * 1024 + 512: dc * 1024 + 1024],
                                     start=(dc == 0), stop=(dc == DC - 1))
                    return r
                t_mm = A("tensor", omm_fn, waits=wt, inc="t")
                if sub == 1:
                    wbfC_free[bsel] = ("t", t_mm[1])
                t_xl = dma("sync", xtile[b2o][:],
                           xown[oc * P:(oc + 1) * P, th * 1024:(th + 1) * 1024],
                           waits=[xt_free[b2o]], q="dxf%d" % b2o)
                wf_ = [c2_start_dout] + ([ft_free[b2o]] if ft_free[b2o] else [])
                t_fl = dma("sync", ftile[b2o][:],
                           ffn_sc[oc * P:(oc + 1) * P, th * 1024:(th + 1) * 1024],
                           waits=wf_, q="dxf%d" % b2o)
                wv_ = [("t", t_mm[1]), t_xl, t_fl]
                if obuf_free[b2o]:
                    wv_.append(dnow())
                def oadd_fn(e, bo=bo, b2o=b2o):
                    e.tensor_tensor(otmp[:], psOv[bo], xtile[b2o][:], ALU.add)
                    return e.tensor_tensor(obuf[b2o][:], otmp[:], ftile[b2o][:], ALU.add)
                t_oa = A("vector", oadd_fn, waits=wv_, inc="v")
                psO_free[bo] = ("v", t_oa[1])
                xt_free[b2o] = ("v", t_oa[1])
                ft_free[b2o] = ("v", t_oa[1])
                t_os = dma("scalar", outf[oc * P:(oc + 1) * P, th * 1024:(th + 1) * 1024],
                           obuf[b2o][:], waits=[("v", t_oa[1])], q="dout")
                obuf_free[b2o] = ("dout", t_os[1])

    if dbg:
        pl.barrier()
        for nm, src, shp, dt in [
            ("d_xn", xn_sc, [D, 4096], BF16),
            ("d_k", k_sc, [NKV * P, 4096], BF16),
            ("d_v", v_sc, [4096, NKV * HD], BF16),
            ("d_q", q_sc, [32 * P, TOWN], BF16),
            ("d_y", y_sc, [D, TOWN], BF16),
            ("d_ffn", ffn_sc, [D, TOWN], F32),
            ("d_attn", attn_sc, [32 * P, TOWN], BF16),
        ]:
            dst = nc.dram_tensor(nm, shp, dt, kind="ExternalOutput").ap()
            dma("sync", dst, src, q="dc")
        d_st = nc.dram_tensor("d_st", [P, 72], F32, kind="ExternalOutput").ap()
        stt = sb("d_stt", [P, 72], F32)
        A("vector", lambda e: (e.tensor_copy(stt[:, 0:8], ksqmax[:]),
                               e.tensor_copy(stt[:, 8:40], qsqmax[:]),
                               e.tensor_copy(stt[:, 40:72], m_bias[:]))[-1], inc="v")
        dma("sync", d_st, stt[:], waits=[("v", pl.cnt["v"])], q="dc")
    final = pl.snap()

    import contextlib
    with contextlib.ExitStack() as stack:
        sems = {}
        for name in pl.cnt:
            sems[name] = stack.enter_context(nc.semaphore("s_" + name))
        block = stack.enter_context(nc.Block())
        own_of = {"tensor": "t", "vector": "v", "scalar": "a", "gpsimd": "p",
                  "sync": None}
        dma_sems = {n for n in pl.cnt if n not in ("t", "v", "a", "p")}

        def emit(eng_name):
            def go(eng):
                own = own_of[eng_name]
                for fn, waits, tgt in pl.ops[eng_name]:
                    for s, val in waits:
                        if s == own:
                            continue
                        eng.wait_ge(sems[s], val)
                    inst = fn(eng) if fn is not None else None
                    if tgt is not None:
                        s, val = tgt
                        if inst is None:
                            inst = eng.nop()
                        inst.then_inc(sems[s], 16 if s in dma_sems else 1)
                if eng_name == "sync":
                    eng.wait_ge(sems["dout"], final["dout"])
            return go

        block.tensor(emit("tensor"))
        block.vector(emit("vector"))
        block.scalar(emit("scalar"))
        block.gpsimd(emit("gpsimd"))
        block.sync(emit("sync"))

    return nc


def _host_tables(j):
    inv = 10000.0 ** (-(np.arange(64, dtype=np.float64) * 2.0 / 128.0))
    pos = np.arange(4096, dtype=np.float64)
    fr = pos[None, :] * inv[:, None]
    cos = np.cos(fr).astype(np.float32)
    sin = np.sin(fr).astype(np.float32)
    cosb = np.concatenate([cos, cos], 0)
    sinx = np.concatenate([-sin, sin], 0)
    own_cols = np.concatenate([np.arange(b * 512, (b + 1) * 512) for b in OWNBLK[j]])
    cosq = np.ascontiguousarray(cosb[:, own_cols])
    sinq = np.ascontiguousarray(sinx[:, own_cols])
    m = np.zeros((32, 128, 512), np.float32)
    for s in range(4):
        ob = OWNBLK[j][s]
        cnt = SLOT_CNT[s]
        qabs = ob * 512 + np.arange(512)
        for w in range(8):
            kb = cnt - 8 + w
            kabs = kb * 128 + np.arange(128)
            m[s * 8 + w] = (kabs[:, None] <= qabs[None, :]).astype(np.float32)
    bf = ml_dtypes.bfloat16
    return (cosb.astype(bf), sinx.astype(bf), cosq.astype(bf), sinq.astype(bf),
            np.ascontiguousarray(m.reshape(32 * 128, 512)).astype(bf), own_cols)


def kernel(L0_attn_x, L0_attn_Wq, L0_attn_Wk, L0_attn_Wv, L0_attn_Wo,
           L0_ffn_weight, L0_ffn_bias, L0_ffn_g, L0_ffn_h,
           L0_ffn_ln_gamma, L0_ffn_ln_beta, _dbg=False):
    x = np.ascontiguousarray(np.asarray(L0_attn_x, dtype=np.float32))
    wq_ = np.ascontiguousarray(np.asarray(L0_attn_Wq, dtype=np.float32))
    wk_ = np.ascontiguousarray(np.asarray(L0_attn_Wk, dtype=np.float32))
    wv_ = np.ascontiguousarray(np.asarray(L0_attn_Wv, dtype=np.float32))
    wo_ = np.ascontiguousarray(np.asarray(L0_attn_Wo, dtype=np.float32))
    wfT_ = np.ascontiguousarray(np.asarray(L0_ffn_weight, dtype=np.float32).T)

    def col(v):
        return np.ascontiguousarray(np.asarray(v, dtype=np.float32).reshape(DC, P).T)

    vecs_ = np.ascontiguousarray(np.concatenate(
        [col(L0_ffn_bias), col(L0_ffn_g), col(L0_ffn_h),
         col(L0_ffn_ln_gamma), col(L0_ffn_ln_beta),
         np.zeros((P, DC), np.float32)], axis=1))

    tabs = [_host_tables(0), _host_tables(1)]
    xT = [np.ascontiguousarray(x[b].T) for b in range(4)]

    nc = build_nc(dbg=_dbg)
    in_maps = []
    for c in range(8):
        b, j = c // 2, c % 2
        cosb_, sinx_, cosq_, sinq_, masks_, own_cols = tabs[j]
        xown_ = np.ascontiguousarray(xT[b][:, own_cols])
        in_maps.append({
            "xT": xT[b], "xown": xown_, "wq": wq_, "wk": wk_, "wv": wv_,
            "wo": wo_, "wfT": wfT_, "vecs": vecs_,
            "cosb": cosb_, "sinx": sinx_, "cosq": cosq_, "sinq": sinq_,
            "masks": masks_,
        })
    res = run_bass_kernel_spmd(nc, in_maps, core_ids=list(range(8)))
    out = np.empty((4, 4096, 4096), np.float32)
    for c in range(8):
        b, j = c // 2, c % 2
        of = np.asarray(res.results[c]["outf"])
        for s in range(4):
            ob = OWNBLK[j][s]
            out[b, ob * 512:(ob + 1) * 512, :] = of[:, s * 512:(s + 1) * 512].T
    if _dbg:
        return out, res
    return out


if __name__ == "__main__":
    import reference as _r
    inp = _r.setup_inputs()
    o = kernel(**{k: np.asarray(v) for k, v in inp.items()})
    e = np.asarray(_r.reference(**inp))
    err = o - e
    print("rvr", float((err ** 2).mean() / (np.asarray(e) ** 2).mean()),
          "max", float(np.abs(err).max()))


# version 10
# speedup vs baseline: 1.1946x; 1.1946x over previous
import math
import numpy as np
import ml_dtypes
import concourse.bass as bass
import concourse.mybir as mybir
from concourse.bass_utils import run_bass_kernel_spmd

F32 = mybir.dt.float32
BF16 = mybir.dt.bfloat16
AF = mybir.ActivationFunctionType
ALU = mybir.AluOpType
AX = mybir.AxisListType.X

D = 4096
DC = 32
P = 128
TOWN = 2048
NKV = 8
HD = 128
SLOT_CNT = (8, 16, 24, 32)
OWNBLK = ((0, 3, 4, 7), (1, 2, 5, 6))
INV_SQRT_HD = 1.0 / math.sqrt(HD)
EPS = 1e-5

ENGINES = ("tensor", "vector", "scalar", "gpsimd", "sync")


class Plan:

    def __init__(self):
        self.ops = {e: [] for e in ENGINES}
        self.cnt = {"t": 0, "v": 0, "a": 0, "p": 0, "dout": 0}

    def ensure(self, name):
        if name not in self.cnt:
            self.cnt[name] = 0

    def add(self, eng, fn, waits=(), inc=None, inc_by=1):
        for s, val in waits:
            self.ensure(s)
            assert val <= self.cnt[s], f"wait {s}>={val} planned only {self.cnt[s]}"
        tgt = None
        if inc is not None:
            self.ensure(inc)
            self.cnt[inc] += inc_by
            tgt = (inc, self.cnt[inc])
        self.ops[eng].append((fn, tuple((s, v) for s, v in waits if v > 0), tgt))
        return tgt

    def snap(self):
        return dict(self.cnt)

    def barrier(self):
        c = self.snap()
        for e in ENGINES:
            self.ops[e].append((None, tuple((s, c[s]) for s in c if c[s] > 0), None))


def build_nc(dbg=False):
    nc = bass.Bass("TRN2", target_bir_lowering=False, debug=False)
    dbg_outs = {}
    def dbg_dram(nm, shp, dt):
        ap = nc.dram_tensor(nm, shp, dt, kind="ExternalOutput").ap()
        dbg_outs[nm] = ap
        return ap

    xT = nc.dram_tensor("xT", [D, 4096], F32, kind="ExternalInput").ap()
    xown = nc.dram_tensor("xown", [D, TOWN], F32, kind="ExternalInput").ap()
    wq = nc.dram_tensor("wq", [D, D], F32, kind="ExternalInput").ap()
    wk = nc.dram_tensor("wk", [D, 1024], F32, kind="ExternalInput").ap()
    wv = nc.dram_tensor("wv", [D, 1024], F32, kind="ExternalInput").ap()
    wo = nc.dram_tensor("wo", [D, D], F32, kind="ExternalInput").ap()
    wfT = nc.dram_tensor("wfT", [D, D], F32, kind="ExternalInput").ap()
    vecs = nc.dram_tensor("vecs", [P, 6 * DC], F32, kind="ExternalInput").ap()
    cosb = nc.dram_tensor("cosb", [P, 4096], BF16, kind="ExternalInput").ap()
    sinx = nc.dram_tensor("sinx", [P, 4096], BF16, kind="ExternalInput").ap()
    cosq = nc.dram_tensor("cosq", [P, TOWN], BF16, kind="ExternalInput").ap()
    sinq = nc.dram_tensor("sinq", [P, TOWN], BF16, kind="ExternalInput").ap()
    masks = nc.dram_tensor("masks", [32 * P, 512], BF16, kind="ExternalInput").ap()
    outf = nc.dram_tensor("outf", [D, TOWN], F32, kind="ExternalOutput").ap()

    xn_sc = nc.dram_tensor("xn_sc", [D, 4096], BF16, kind="Internal").ap()
    k_sc = nc.dram_tensor("k_sc", [NKV * P, 4096], BF16, kind="Internal").ap()
    v_sc = nc.dram_tensor("v_sc", [4096, NKV * HD], BF16, kind="Internal").ap()
    q_sc = nc.dram_tensor("q_sc", [32 * P, TOWN], BF16, kind="Internal").ap()
    y_sc = nc.dram_tensor("y_sc", [D, TOWN], BF16, kind="Internal").ap()
    ffn_sc = nc.dram_tensor("ffn_sc", [D, TOWN], F32, kind="Internal").ap()
    attn_sc = nc.dram_tensor("attn_sc", [32 * P, TOWN], BF16, kind="Internal").ap()

    xT3 = xT.rearrange("(c p) t -> p c t", p=P)
    xown3 = xown.rearrange("(c p) t -> p c t", p=P)
    xn3 = xn_sc.rearrange("(c p) t -> p c t", p=P)
    wq3 = wq.rearrange("(c p) o -> p c o", p=P)
    wf3 = wfT.rearrange("(c p) o -> p c o", p=P)
    wo3 = wo.rearrange("(c p) o -> p c o", p=P)
    wk3 = wk.rearrange("(c p) o -> p c o", p=P)
    wv3 = wv.rearrange("(c p) o -> p c o", p=P)
    k3 = k_sc.rearrange("(h p) t -> p h t", p=P)
    v3 = v_sc.rearrange("(kb p) c -> p kb c", p=P)
    masks3 = masks.rearrange("(m p) c -> p m c", p=P)

    probe_a = nc.alloc_sbuf_tensor("arena_probe", [P, 32], mybir.dt.uint8)
    base0 = nc.lookup_mloc(probe_a).addr + 32
    ARENA = 229376 - base0 - 2048
    arena = nc.alloc_sbuf_tensor("arena", [P, ARENA], mybir.dt.uint8)
    base = nc.lookup_mloc(arena).addr
    assert base == base0, (base, base0)
    cursor = [base]

    def sb(name, shape, dtype):
        nbytes = shape[1] * (2 if dtype == BF16 else 4)
        at = cursor[0]
        cursor[0] = at + ((nbytes + 31) // 32) * 32
        assert cursor[0] <= base + ARENA, (name, cursor[0])
        return nc.alloc_sbuf_tensor_at(name, shape, dtype, offset=at)

    ones_b = sb("ones_b", [P, 128], BF16)
    ones_f = sb("ones_f", [P, 4], F32)
    vecs_sb = sb("vecs_sb", [P, 6 * DC], F32)
    g_col = sb("g_col", [P, DC], F32)
    h_col = sb("h_col", [P, DC], F32)
    bias_col = sb("bias_col", [P, DC], F32)
    gam_col = sb("gam_col", [P, DC], F32)
    bet_col = sb("bet_col", [P, DC], F32)
    bh_col = sb("bh_col", [P, DC], F32)
    ksqmax = sb("ksqmax", [P, 8], F32)
    qsqmax = sb("qsqmax", [P, 32], F32)
    norms = sb("norms", [P, 256], F32)
    m_bias = sb("m_bias", [P, 32], F32)
    strip = sb("strip", [P, 2048], F32)
    stripb = sb("stripb", [P, 1024], BF16)
    vstage = sb("vstage", [P, 1024], F32)
    PERSIST_END = cursor[0]
    assert PERSIST_END < 44 * 1024, PERSIST_END

    wk_sb = sb("wk_sb", [P, DC * 1024], BF16)
    wv_sb = sb("wv_sb", [P, DC * 1024], BF16)
    A1_DYN = cursor[0]
    xst = [sb(f"xst{i}", [P, D], F32) for i in range(2)]
    xbA = [sb(f"xbA{i}", [P, D], BF16) for i in range(2)]
    xnt = sb("xnt", [P, D], BF16)
    xsqA = xnt
    abc_sb = sb("abc_sb", [P, 128], BF16)
    bbc_sb = sb("bbc_sb", [P, 128], BF16)
    A1A_END = cursor[0]
    cursor[0] = A1_DYN
    cos_sb = sb("cos_sb", [P, 4096], BF16)
    sin_sb = sb("sin_sb", [P, 4096], BF16)
    xng = [sb("xng0", [P, DC * 256], BF16)] * 2
    kf = [sb(f"kf{i}", [P, 256], BF16) for i in range(2)]
    kst = [sb(f"kst{i}", [P, NKV * 256], BF16) for i in range(2)]
    ksqt = [sb(f"ksqt{i}", [P, 256], BF16) for i in range(2)]
    vbb = [sb(f"vbb{i}", [P, 1024], BF16) for i in range(2)]
    t1p = sb("t1p", [P, 256], F32)
    t2p = sb("t2p", [P, 256], F32)
    A1B_END = cursor[0]
    assert max(A1A_END, A1B_END) <= 229376, (A1A_END, A1B_END)

    cursor[0] = PERSIST_END
    xn_own = sb("xn_own", [P, DC * 1024], BF16)
    wstg = [sb(f"wstg{i}", [P, DC * 128], F32) for i in range(2)]
    wbf = [sb(f"wbf{i}", [P, DC * 256], BF16) for i in range(2)]
    cosq_sb = sb("cosq_sb", [P, TOWN], BF16)
    sinq_sb = sb("sinq_sb", [P, TOWN], BF16)
    qf_sb = sb("qf_sb", [P, 1024], BF16)
    qrot = [sb(f"qrot{i}", [P, 1024], BF16) for i in range(2)]
    qsq_sb = sb("qsq_sb", [P, 1024], BF16)
    yb = [sb(f"yb{i}", [P, 1024], BF16) for i in range(2)]
    xstA2 = sb("xstA2", [P, D], F32)
    xbA2 = sb("xbA2", [P, D], BF16)
    xsqA2 = sb("xsqA2", [P, D], BF16)
    t1q = sb("t1q", [P, 1024], F32)
    t2q = sb("t2q", [P, 1024], F32)
    A2_END = cursor[0]
    assert A2_END <= 229376, A2_END

    cursor[0] = PERSIST_END
    masks_sb = sb("masks_sb", [P, 32 * 512], BF16)
    kbuf = [sb(f"kbuf{i}", [P, 4096], BF16) for i in range(2)]
    vbuf = [sb(f"vbuf{i}", [P, 4096], BF16) for i in range(2)]
    qbuf = [sb(f"qbuf{i}", [P, TOWN], BF16) for i in range(2)]
    pT = [sb(f"pT{i}", [P, 1024], BF16) for i in range(4)]
    lacc = [sb(f"lacc{i}", [P, 512], F32) for i in range(2)]
    linv_bc = sb("linv_bc", [P, 512], F32)
    aout = [sb(f"aout{i}", [P, 512], BF16) for i in range(2)]
    B_END = cursor[0]
    assert B_END <= 229376, B_END

    cursor[0] = PERSIST_END
    abuf = sb("abuf", [P, DC * 1024], BF16)
    wstgC = [sb(f"wstgC{i}", [P, DC * 128], F32) for i in range(2)]
    wbfC = [sb(f"wbfC{i}", [P, DC * 256], BF16) for i in range(2)]
    xtile = [sb(f"xtile{i}", [P, 1024], F32) for i in range(2)]
    otmp = sb("otmp", [P, 1024], F32)
    obuf = [sb(f"obuf{i}", [P, 1024], F32) for i in range(2)]
    ybuf2 = [sb(f"ybuf2{i}", [P, 1024], BF16) for i in range(2)]
    ysq2 = [sb(f"ysq2{i}", [P, 1024], BF16) for i in range(2)]
    fbuf = [sb(f"fbuf{i}", [P, 1024], F32) for i in range(2)]
    C2_END = cursor[0]
    assert C2_END <= 229376, C2_END

    psA = nc.alloc_psum_tensor("psA", [P, 2048], F32)
    psB = nc.alloc_psum_tensor("psB", [P, 2048], F32)

    pl = Plan()
    A = pl.add

    def dma(eng, out_ap, in_ap, waits=(), q="dout"):
        def fn(e, out_ap=out_ap, in_ap=in_ap):
            return e.dma_start(out=out_ap, in_=in_ap)
        return A(eng, fn, waits=[w for w in waits if w], inc=q, inc_by=16)

    def dnow():
        return ("dout", pl.cnt.get("dout", 0))

    def bcast_over(handle, ap2, reps):
        return bass.AP(handle, ap2.offset,
                       [list(ap2.ap[0]), [0, reps], list(ap2.ap[1])])

    def chunk_bcast(handle, ap2, inner):
        return bass.AP(handle, ap2.offset,
                       [list(ap2.ap[0]), list(ap2.ap[1]), [0, inner]])

    t_vec = dma("sync", vecs_sb[:], vecs, q="dc")
    A("vector", lambda e: e.memset(ones_b[:], 1.0))
    A("vector", lambda e: e.memset(ones_f[:], 1.0))
    A("vector", lambda e: e.memset(ksqmax[:], 0.0))
    A("vector", lambda e: e.memset(qsqmax[:], 0.0))
    A("vector", lambda e: e.tensor_copy(bias_col[:], vecs_sb[:, 0:DC]), waits=[t_vec])
    A("vector", lambda e: e.tensor_copy(g_col[:], vecs_sb[:, DC:2 * DC]))
    A("vector", lambda e: e.tensor_copy(h_col[:], vecs_sb[:, 2 * DC:3 * DC]))
    A("vector", lambda e: e.tensor_copy(gam_col[:], vecs_sb[:, 3 * DC:4 * DC]))
    A("vector", lambda e: e.tensor_copy(bet_col[:], vecs_sb[:, 4 * DC:5 * DC]))
    A("vector", lambda e: e.tensor_mul(bh_col[:], bias_col[:], h_col[:]), inc="v")

    xb_free = [None, None]
    xnt_free = [None, None]
    stats_free = None
    bca_free = None
    vstage_free = [None]
    wkv_cast_last = None
    xn_written = {}

    wkv_iter = [("wk", dc) for dc in range(DC)] + [("wv", dc) for dc in range(DC)]
    wkv_pos = 0

    def plan_wkv(n):
        nonlocal wkv_pos, wkv_cast_last
        for _ in range(n):
            if wkv_pos >= len(wkv_iter):
                return
            which, dc = wkv_iter[wkv_pos]
            src = (wk3 if which == "wk" else wv3)[:, dc:dc + 1, :]
            dst = vstage[:].unsqueeze(1)
            tgt = dma("sync", dst, src, waits=[vstage_free[0]], q="dw")
            wsb = wk_sb if which == "wk" else wv_sb
            def cast_fn(e, wsb=wsb, dc=dc):
                return e.tensor_copy(wsb[:, dc * 1024:(dc + 1) * 1024], vstage[:])
            c = A("vector", cast_fn, waits=[tgt], inc="v")
            vstage_free[0] = c
            wkv_cast_last = c
            wkv_pos += 1

    def plan_ln_tile(src3, tcols, xst_, xb_, xsq_, stats_ps, bc_ps, pool_dst_fn,
                     war_load, war_cast, war_pool, ldq):
        nonlocal_waits = [w for w in war_load if w]
        t_x = dma("sync", xst_[:].rearrange("p (c t) -> p c t", c=DC),
                  src3[:, :, tcols], waits=nonlocal_waits, q=ldq)
        wcast = [t_x] + [w for w in war_cast if w]
        A("vector", lambda e: e.tensor_copy(xb_[:], xst_[:]), waits=wcast)
        t_xb = A("vector", lambda e: e.tensor_mul(xsq_[:], xb_[:], xb_[:]), inc="v")
        wt = [("v", t_xb[1]), stats_free] if stats_free else [("v", t_xb[1])]
        def stats_fn(e):
            for dc in range(DC):
                e.matmul(stats_ps[0:1, 0:128], ones_b[:, 0:1],
                         xb_[:, dc * 128:(dc + 1) * 128], start=(dc == 0), stop=(dc == DC - 1))
            for dc in range(DC):
                r = e.matmul(stats_ps[0:1, 128:256], ones_b[:, 0:1],
                             xsq_[:, dc * 128:(dc + 1) * 128], start=(dc == 0), stop=(dc == DC - 1))
            return r
        t_st = A("tensor", stats_fn, waits=wt, inc="t")
        def strip_fn(e):
            e.tensor_scalar_mul(strip[0:1, 0:128], stats_ps[0:1, 0:128], 1.0 / D)
            e.tensor_scalar_mul(strip[0:1, 128:256], stats_ps[0:1, 128:256], 1.0 / D)
            e.tensor_mul(strip[0:1, 256:384], strip[0:1, 0:128], strip[0:1, 0:128])
            e.tensor_sub(strip[0:1, 256:384], strip[0:1, 128:256], strip[0:1, 256:384])
            return e.tensor_scalar_add(strip[0:1, 256:384], strip[0:1, 256:384], EPS)
        t_sp = A("vector", strip_fn, waits=[("t", t_st[1])], inc="v")
        def act_fn(e):
            return e.activation(strip[0:1, 640:768], strip[0:1, 256:384], AF.Sqrt)
        t_ac = A("scalar", act_fn, waits=[("v", t_sp[1])], inc="a")
        def strip2_fn(e):
            e.reciprocal(strip[0:1, 384:512], strip[0:1, 640:768])
            e.tensor_mul(strip[0:1, 512:640], strip[0:1, 0:128], strip[0:1, 384:512])
            e.tensor_copy(stripb[0:1, 0:128], strip[0:1, 384:512])
            return e.tensor_copy(stripb[0:1, 128:256], strip[0:1, 512:640])
        t_s2 = A("vector", strip2_fn, waits=[("a", t_ac[1])], inc="v")
        wt2 = [("v", t_s2[1])] + ([bca_free] if bca_free else [])
        def bc_fn(e):
            e.matmul(bc_ps[:, 0:128], ones_b[0:1, :], stripb[0:1, 0:128], start=True, stop=True)
            return e.matmul(bc_ps[:, 128:256], ones_b[0:1, :], stripb[0:1, 128:256], start=True, stop=True)
        t_bc = A("tensor", bc_fn, waits=wt2, inc="t")
        def cpbc_fn(e):
            e.tensor_copy(abc_sb2[:], bc_ps[:, 0:128])
            return e.tensor_copy(bbc_sb2[:], bc_ps[:, 128:256])
        t_cp = A("vector", cpbc_fn, waits=[("t", t_bc[1])], inc="v")
        wp = [("v", t_cp[1])] + [w for w in war_pool if w]
        def pool_fn(e):
            a_b = bcast_over(abc_sb2, abc_sb2[:], DC)
            b_b = bcast_over(bbc_sb2, bbc_sb2[:], DC)
            return pool_dst_fn(e, a_b, b_b)
        t_pl = A("vector", pool_fn, waits=wp, inc="v")
        return t_pl, t_xb, ("v", t_sp[1]), ("v", t_pl[1])

    abc_sb2 = abc_sb
    bbc_sb2 = bbc_sb

    plan_wkv(8)
    sq_prev = None
    xnt_free1 = None
    cast_prev = [None, None]
    for i in range(32):
        b = i % 2
        def mk_pool_dst(b=b):
            def f(e, a_b, b_b):
                src = xbA[b][:].rearrange("p (c t) -> p c t", c=DC)
                dst = xnt[:].rearrange("p (c t) -> p c t", c=DC)
                e.tensor_tensor(dst, src, a_b, ALU.mult)
                return e.tensor_tensor(dst, dst, b_b, ALU.subtract)
            return f
        t_pl, sq_prev, stats_free, bca_free = plan_ln_tile(
            xT3, slice(i * 128, (i + 1) * 128), xst[b], xbA[b], xsqA,
            psA[:, 0:512], psA[:, 512:1024], mk_pool_dst(),
            war_load=[cast_prev[b]], war_cast=[xb_free[b], xnt_free1],
            war_pool=[xnt_free1], ldq="dx")
        cast_prev[b] = sq_prev
        xb_free[b] = bca_free
        t_wr = dma("scalar", xn3[:, :, i * 128:(i + 1) * 128],
                   xnt[:].rearrange("p (c t) -> p c t", c=DC),
                   waits=[bca_free], q="dout")
        xnt_free1 = ("dout", t_wr[1])
        xn_written[i] = t_wr[1]
        if dbg and i == 31:
            d1 = dbg_dram("d_xsq31", [P, D], BF16)
            d2 = dbg_dram("d_strip31", [1, 1024], F32)
            d3 = dbg_dram("d_xb31", [P, D], BF16)
            d4 = dbg_dram("d_abc31", [P, 128], BF16)
            d5 = dbg_dram("d_bbc31", [P, 128], BF16)
            dma("sync", d1, xsqA[:], waits=[stats_free], q="dc")
            dma("sync", d2, strip[0:1, 0:1024], waits=[stats_free, bca_free], q="dc")
            dma("sync", d3, xbA[i % 2][:], waits=[stats_free], q="dc")
            dma("sync", d4, abc_sb[:], waits=[bca_free])
            dma("sync", d5, bbc_sb[:], waits=[bca_free])
        plan_wkv(2)
    plan_wkv(len(wkv_iter) - wkv_pos)

    pl.barrier()

    t_cos = dma("sync", cos_sb[:], cosb, q="dc")
    t_sin = dma("sync", sin_sb[:], sinx, q="dc")
    GTOK = 256
    NG = 4096 // GTOK
    psK_free = [None, None]
    psV_free = [None, None]
    xng_free = [None, None]
    kf_free = [None, None]
    kst_free = [None, None]
    ksqt_free = [None, None]
    vbb_free = [None, None]
    kv_written = []
    psKv = [psA[:, 1024:1280], psA[:, 1536:1792]]
    psKstrip = [psA[0:1, 1024:1280], psA[0:1, 1536:1792]]
    psVv = [psB[:, 0:1024], psB[:, 1024:2048]]

    for g in range(NG):
        gc = slice(g * GTOK, (g + 1) * GTOK)
        bg = g % 2
        w = [("dout", xn_written[(g * GTOK + GTOK) // 128 - 1]), xng_free[0]]
        if g == 0:
            w += [t_sin, wkv_cast_last]
        t_xn = dma("sync", xng[bg][:].rearrange("p (c t) -> p c t", c=DC),
                   xn3[:, :, gc], waits=w, q="dg")
        last_rp = None
        pending_kq = [None]
        def drain_kq():
            if pending_kq[0] is not None:
                pending_kq[0]()
                pending_kq[0] = None
        for kh in range(NKV):
            bk = (g * NKV + kh) % 2
            wt = [t_xn] + ([psK_free[bk]] if psK_free[bk] else [])
            def kmm_fn(e, bk=bk, kh=kh, bg=bg):
                for dc in range(DC):
                    r = e.matmul(psKv[bk],
                                 wk_sb[:, dc * 1024 + kh * 128: dc * 1024 + (kh + 1) * 128],
                                 xng[bg][:, dc * GTOK:(dc + 1) * GTOK],
                                 start=(dc == 0), stop=(dc == DC - 1))
                return r
            t_k = A("tensor", kmm_fn, waits=wt, inc="t")
            drain_kq()
            wv_ = [("t", t_k[1])]
            if kh == 0 and kst_free[bg]:
                wv_.append(dnow())
            def rope_fn(e, bk=bk, kh=kh, bg=bg, gc=gc):
                dst = kst[bg][:, kh * 256:(kh + 1) * 256]
                e.tensor_tensor(t1p[:], psKv[bk], cos_sb[:, gc], ALU.mult)
                e.tensor_tensor(t2p[0:64, :], psKv[bk][64:128, :], sin_sb[0:64, gc], ALU.mult)
                e.tensor_tensor(t2p[64:128, :], psKv[bk][0:64, :], sin_sb[64:128, gc], ALU.mult)
                return e.tensor_tensor(dst, t1p[:], t2p[:], ALU.add)
            t_rp = A("vector", rope_fn, waits=wv_, inc="v")
            wp = [("v", t_rp[1])] + ([ksqt_free[bk]] if ksqt_free[bk] else [])
            def ksq_fn(e, bk=bk, kh=kh, bg=bg):
                dst = kst[bg][:, kh * 256:(kh + 1) * 256]
                return e.tensor_tensor(ksqt[bk][:], dst, dst, ALU.mult)
            t_sq3 = A("gpsimd", ksq_fn, waits=wp, inc="p")
            def deferred(bk=bk, kh=kh, g=g, t_sq3=t_sq3, t_rp=t_rp):
                t_kq = A("tensor", lambda e, bk=bk: e.matmul(psKstrip[bk], ones_b[:, 0:1],
                                                             ksqt[bk][:], start=True, stop=True),
                         waits=[("p", t_sq3[1]), ("v", t_rp[1])], inc="t")
                def kmax_fn(e, bk=bk, kh=kh, g=g):
                    return e.reduce_max(norms[0:1, kh * 16 + g: kh * 16 + g + 1],
                                        psKstrip[bk], axis=AX)
                t_km = A("vector", kmax_fn, waits=[("t", t_kq[1])], inc="v")
                psK_free[bk] = ("v", t_km[1])
                ksqt_free[bk] = ("v", t_km[1])
            pending_kq[0] = deferred
            last_rp = t_rp
        if dbg and g == 0:
            d6 = dbg_dram("d_wksb", [P, DC * 1024], BF16)
            dma("sync", d6, wk_sb[:], waits=[wkv_cast_last], q="dc")
            d7 = dbg_dram("d_kst0", [P, NKV * 256], BF16)
            dma("sync", d7, kst[bg][:], waits=[("v", last_rp[1])], q="dc")
        drain_kq()
        t_ks = dma("scalar", k3[:, :, gc], kst[bg][:].rearrange("p (h t) -> p h t", h=NKV),
                   waits=[("v", last_rp[1])], q="dout")
        kst_free[bg] = ("dout", t_ks[1])
        kv_written.append(t_ks[1])
        for th in range(2):
            bvp = (g * 2 + th) % 2
            wt = [t_xn] + ([psV_free[bvp]] if psV_free[bvp] else [])
            def vmm_fn(e, bvp=bvp, th=th, bg=bg):
                for dc in range(DC):
                    e.matmul(psVv[bvp][:, 0:512],
                             xng[bg][:, dc * GTOK + th * 128: dc * GTOK + th * 128 + 128],
                             wv_sb[:, dc * 1024: dc * 1024 + 512],
                             start=(dc == 0), stop=(dc == DC - 1))
                for dc in range(DC):
                    r = e.matmul(psVv[bvp][:, 512:1024],
                                 xng[bg][:, dc * GTOK + th * 128: dc * GTOK + th * 128 + 128],
                                 wv_sb[:, dc * 1024 + 512: dc * 1024 + 1024],
                                 start=(dc == 0), stop=(dc == DC - 1))
                return r
            t_v = A("tensor", vmm_fn, waits=wt, inc="t")
            wv_ = [("t", t_v[1])] + ([dnow()] if vbb_free[bvp] else [])
            t_vc = A("vector", lambda e, bvp=bvp: e.tensor_copy(vbb[bvp][:], psVv[bvp]),
                     waits=wv_, inc="v")
            psV_free[bvp] = ("v", t_vc[1])
            t_vs = dma("scalar", v_sc[(g * GTOK + th * 128):(g * GTOK + th * 128 + 128), :],
                       vbb[bvp][:], waits=[("v", t_vc[1])], q="dout")
            vbb_free[bvp] = ("dout", t_vs[1])
            kv_written.append(t_vs[1])
        xng_free[0] = ("t", pl.cnt["t"])

    pl.barrier()

    t_cq = dma("sync", cosq_sb[:], cosq, q="dc")
    t_sq = dma("sync", sinq_sb[:], sinq, q="dc")
    xbA2_free = None
    wstg_free = [None, None]
    wbf_free = [None, None]
    psQ_free = [None, None, None]
    qsq_free = None
    qidx = [0]
    qrot_free = [None, None]
    qrot_free2 = [None, None]
    yb_free = [None, None]
    q_written = {}
    y_written = {}
    psQv = [psA[:, 0:1024], psA[:, 1024:2048], psB[:, 1024:2048]]
    abc_sb2 = abc_sb
    bbc_sb2 = bbc_sb
    stats_free = None
    bca_free = None

    sqA2_prev = None
    for th in range(2):
        xn_ready = None
        xn_half = [None, None]
        for j in range(8):
            def mk_pool_dst(j=j):
                def f(e, a_b, b_b):
                    dst = bass.AP(xn_own, xn_own[:].offset + j * 128,
                                  [list(xn_own[:].ap[0]), [1024, DC], [1, 128]])
                    src = xbA2[:].rearrange("p (c t) -> p c t", c=DC)
                    e.tensor_tensor(dst, src, a_b, ALU.mult)
                    return e.tensor_tensor(dst, dst, b_b, ALU.subtract)
                return f
            war_pool = [xbA2_free]
            if th == 1 and j == 0:
                war_pool.append(("t", pl.cnt["t"]))
            t_pl, sqA2_prev, stats_free, bca_free = plan_ln_tile(
                xown3, slice(th * 1024 + j * 128, th * 1024 + (j + 1) * 128),
                xstA2, xbA2, xsqA2,
                psB[:, 0:512], psB[:, 512:1024], mk_pool_dst(),
                war_load=[sqA2_prev], war_cast=[xbA2_free], war_pool=war_pool, ldq="dx2")
            xbA2_free = bca_free
            xn_ready = bca_free
            if j == 3:
                xn_half[0] = bca_free
            if j == 7:
                xn_half[1] = bca_free
        pending_qn = [None]
        def drain_qn():
            if pending_qn[0] is not None:
                pending_qn[0]()
                pending_qn[0] = None
        for w2 in range(32):
            is_q = w2 < 16
            duo = w2 if is_q else w2 - 16
            bsel = w2 % 2
            cast_tgts = [None, None]
            for half in range(2):
                src3 = (wq3 if is_q else wf3)[:, :, duo * 256 + half * 128: duo * 256 + (half + 1) * 128]
                t_w = dma("sync", wstg[half][:].rearrange("p (c f) -> p c f", c=DC), src3,
                          waits=[wstg_free[half]], q="dw2")
                wc = [t_w] + ([wbf_free[bsel]] if (wbf_free[bsel] and half == 0) else [])
                dst3 = bass.AP(wbf[bsel], wbf[bsel][:].offset + half * 128,
                               [list(wbf[bsel][:].ap[0]), [256, DC], [1, 128]])
                if is_q:
                    def cast_fn(e, bsel=bsel, half=half, dst3=dst3):
                        return e.tensor_copy(dst3, wstg[half][:].rearrange("p (c f) -> p c f", c=DC))
                    t_cast = A("vector", cast_fn, waits=wc, inc="v")
                    wstg_free[half] = t_cast
                else:
                    def sgn_fn(e, bsel=bsel, half=half, dst3=dst3):
                        return e.activation(dst3, wstg[half][:].rearrange("p (c f) -> p c f", c=DC),
                                            AF.Sign)
                    t_sgn = A("scalar", sgn_fn, waits=wc, inc="a")
                    wstg_free[half] = t_sgn
                    def gmul_fn(e, bsel=bsel, dst3=dst3):
                        g_b = chunk_bcast(g_col, g_col[:], 128)
                        return e.tensor_tensor(dst3, dst3, g_b, ALU.mult)
                    t_cast = A("vector", gmul_fn, waits=[t_sgn], inc="v")
                cast_tgts[half] = t_cast
            for sub in range(2):
                oc = duo * 2 + sub
                bq = qidx[0] % 3
                b2 = qidx[0] % 2
                qidx[0] += 1
                t_cast = cast_tgts[sub]
                wt = [("v", t_cast[1]), xn_half[0]] + ([psQ_free[bq]] if psQ_free[bq] else [])
                def wmm_fn(e, bsel=bsel, sub=sub, bq=bq):
                    for dc in range(DC):
                        r = e.matmul(psQv[bq][:, 0:512],
                                     wbf[bsel][:, dc * 256 + sub * 128: dc * 256 + sub * 128 + 128],
                                     xn_own[:, dc * 1024: dc * 1024 + 512],
                                     start=(dc == 0), stop=(dc == DC - 1))
                    return r
                t_mma = A("tensor", wmm_fn, waits=wt, inc="t")
                def wmm_fn2(e, bsel=bsel, sub=sub, bq=bq):
                    for dc in range(DC):
                        r = e.matmul(psQv[bq][:, 512:1024],
                                     wbf[bsel][:, dc * 256 + sub * 128: dc * 256 + sub * 128 + 128],
                                     xn_own[:, dc * 1024 + 512: dc * 1024 + 1024],
                                     start=(dc == 0), stop=(dc == DC - 1))
                    return r
                t_mm = A("tensor", wmm_fn2, waits=[xn_half[1]], inc="t")
                drain_qn()
                if sub == 1:
                    wbf_free[bsel] = ("t", t_mm[1])
                if is_q:
                    h = oc
                    wv_ = [("t", t_mm[1]), t_cq, t_sq]
                    if qrot_free[b2]:
                        wv_.append(dnow())
                    if qrot_free2[b2]:
                        wv_.append(qrot_free2[b2])
                    def qrope_fn(e, bq=bq, b2=b2, th=th):
                        cs = slice(th * 1024, (th + 1) * 1024)
                        e.tensor_tensor(t1q[:], psQv[bq], cosq_sb[:, cs], ALU.mult)
                        e.tensor_tensor(t2q[0:64, :], psQv[bq][64:128, :], sinq_sb[0:64, cs], ALU.mult)
                        e.tensor_tensor(t2q[64:128, :], psQv[bq][0:64, :], sinq_sb[64:128, cs], ALU.mult)
                        return e.tensor_tensor(qrot[b2][:], t1q[:], t2q[:], ALU.add)
                    t_rv = A("vector", qrope_fn, waits=wv_, inc="v")
                    wp = [("v", t_rv[1])] + ([qsq_free] if qsq_free else [])
                    def qsq_fn(e, b2=b2):
                        return e.tensor_tensor(qsq_sb[:], qrot[b2][:], qrot[b2][:], ALU.mult)
                    t_rp = A("gpsimd", qsq_fn, waits=wp, inc="p")
                    def deferred_qn(bq=bq, h=h, th=th, t_rp=t_rp, t_rv=t_rv):
                        def qn_fn(e, bq=bq):
                            e.matmul(psQv[bq][0:1, 0:512], ones_b[:, 0:1], qsq_sb[:, 0:512],
                                     start=True, stop=True)
                            return e.matmul(psQv[bq][0:1, 512:1024], ones_b[:, 0:1],
                                            qsq_sb[:, 512:1024], start=True, stop=True)
                        t_qn = A("tensor", qn_fn, waits=[("p", t_rp[1]), ("v", t_rv[1])], inc="t")
                        def qmax_fn(e, bq=bq, h=h, th=th):
                            e.tensor_copy(strip[0:1, 1024:1536], psQv[bq][0:1, 0:512])
                            e.tensor_copy(strip[0:1, 1536:2048], psQv[bq][0:1, 512:1024])
                            return e.reduce_max(norms[0:1, 128 + h * 2 + th: 129 + h * 2 + th],
                                                strip[0:1, 1024:2048], axis=AX)
                        t_qm = A("vector", qmax_fn, waits=[("t", t_qn[1])], inc="v")
                        psQ_free[bq] = ("v", t_qm[1])
                        globals()
                        nonlocal qsq_free
                        qsq_free = ("t", t_qn[1])
                    pending_qn[0] = deferred_qn
                    t_qs = dma("scalar", q_sc[h * P:(h + 1) * P, th * 1024:(th + 1) * 1024],
                               qrot[b2][:], waits=[("v", t_rv[1])], q="dout")
                    qrot_free[b2] = ("dout", t_qs[1])
                    qrot_free2[b2] = ("p", t_rp[1])
                    q_written[(h, th)] = t_qs[1]
                else:
                    o = oc
                    wv_ = [("t", t_mm[1])] + ([dnow()] if yb_free[b2] else [])
                    def yevac_fn(e, bq=bq, b2=b2, o=o):
                        return e.tensor_scalar(yb[b2][:], psQv[bq], h_col[:, o:o + 1],
                                               bh_col[:, o:o + 1], ALU.mult, ALU.add)
                    t_ye = A("vector", yevac_fn, waits=wv_, inc="v")
                    psQ_free[bq] = ("v", t_ye[1])
                    t_ys = dma("scalar", y_sc[o * P:(o + 1) * P, th * 1024:(th + 1) * 1024],
                               yb[b2][:], waits=[("v", t_ye[1])], q="dout")
                    yb_free[b2] = ("dout", t_ys[1])
                    y_written[(o, th)] = t_ys[1]

        drain_qn()

    pl.barrier()

    pl.barrier()

    b_start_dout = dnow()
    t_msk = dma("sync", masks_sb[:].rearrange("p (m c) -> p m c", m=32), masks3, q="dc")
    def mprep_fn(e):
        for kh in range(8):
            e.reduce_max(ksqmax[0:1, kh:kh + 1], norms[0:1, kh * 16:(kh + 1) * 16], axis=AX)
        ap0 = bass.AP(norms, norms[:].offset + 128, [list(norms[:].ap[0]), [2, 32]])
        ap1 = bass.AP(norms, norms[:].offset + 129, [list(norms[:].ap[0]), [2, 32]])
        return e.tensor_tensor(qsqmax[0:1, 0:32], ap0[0:1], ap1[0:1], ALU.max)
    t_mp = A("vector", mprep_fn, waits=[], inc="v")
    def mmul_fn(e):
        r = None
        for kh in range(8):
            kb = bass.AP(ksqmax, ksqmax[:].offset + kh, [list(ksqmax[:].ap[0]), [0, 4]])
            r = e.tensor_tensor(strip[0:1, kh * 4:(kh + 1) * 4],
                                qsqmax[0:1, kh * 4:(kh + 1) * 4], kb[0:1], ALU.mult)
        return r
    t_mm2 = A("gpsimd", mmul_fn, waits=[("v", t_mp[1])], inc="p")
    def mact_fn(e):
        return e.activation(strip[0:1, 32:64], strip[0:1, 0:32], AF.Sqrt, scale=1.0 / HD)
    t_ma = A("scalar", mact_fn, waits=[("p", t_mm2[1])], inc="a")
    t_mn = A("vector", lambda e: e.tensor_copy(stripb[0:1, 0:32], strip[0:1, 32:64]),
             waits=[("a", t_ma[1])], inc="v")
    t_mb = A("tensor", lambda e: e.matmul(psB[:, 0:32], ones_b[0:1, :], stripb[0:1, 0:32],
                                          start=True, stop=True),
             waits=[("v", t_mn[1])], inc="t")
    t_mc = A("vector", lambda e: e.tensor_scalar_mul(m_bias[:], psB[:, 0:32], -1.0),
             waits=[("t", t_mb[1])], inc="v")

    kbuf_free = [None, None]
    vbuf_free = [None, None]
    qbuf_free = [None, None]
    pT_free = [None, None, None, None]
    lacc_free = [None, None]
    aout_free = [None, None]
    psS_free = [None, None]
    psPV_free = [None, None]
    lsum_free = None
    lbc_free = ("v", t_mc[1])

    psSv = [psA[:, 0:1024], psA[:, 1024:2048]]
    psPVv = [psB[:, 0:512], psB[:, 512:1024]]
    psLSv = [psB[0:1, 1024:1536], psB[0:1, 1536:2048]]
    psLBv = [psB[:, 1024:1536], psB[:, 1536:2048]]
    lsum_free2 = [None, None]
    attn_written = {}
    kv_done = b_start_dout
    gj = 0
    hs = 0

    for kh in range(NKV):
        bk = kh % 2
        wk_ = [kv_done] + ([kbuf_free[bk]] if kbuf_free[bk] else [])
        t_kl = dma("sync", kbuf[bk][:], k_sc[kh * P:(kh + 1) * P, :], waits=wk_,
                   q="dkv%d" % bk)
        wv_ = [kv_done] + ([vbuf_free[bk]] if vbuf_free[bk] else [])
        t_vl = dma("sync", vbuf[bk][:].rearrange("p (kb c) -> p kb c", kb=32),
                   v3[:, :, kh * HD:(kh + 1) * HD], waits=wv_, q="dkv%d" % bk)
        for hh in range(4):
            h = kh * 4 + hh
            bq = h % 2
            wq_ = [b_start_dout]
            if qbuf_free[bq]:
                wq_.append(qbuf_free[bq])
            t_ql = dma("sync", qbuf[bq][:], q_sc[h * P:(h + 1) * P, :], waits=wq_,
                       q="dq%d" % bq)
            for s in range(4):
                cnt = SLOT_CNT[s]
                hs_cur = hs
                pv_slot = psPVv[hs_cur % 2]
                G = cnt // 2
                sc_tgt = [None] * G
                sc_slot = [None] * G

                def plan_scores(j, bk=bk, bq=bq, s=s):
                    bs = (gj0 + j) % 2
                    kb0, kb1 = 2 * j, 2 * j + 1
                    wt = [t_kl, t_vl, t_ql]
                    if psS_free[bs]:
                        wt.append(psS_free[bs])
                    if hs == 0 and j == 0:
                        wt.append(("v", t_mc[1]))
                    def smm_fn(e, bs=bs, kb0=kb0, kb1=kb1, bk=bk, bq=bq, s=s):
                        qv_ = qbuf[bq][:, s * 512:(s + 1) * 512]
                        e.matmul(psSv[bs][:, 0:512], kbuf[bk][:, kb0 * 128:(kb0 + 1) * 128],
                                 qv_, start=True, stop=True)
                        return e.matmul(psSv[bs][:, 512:1024], kbuf[bk][:, kb1 * 128:(kb1 + 1) * 128],
                                        qv_, start=True, stop=True)
                    sc_tgt[j] = A("tensor", smm_fn, waits=wt, inc="t")
                    sc_slot[j] = bs

                gj0 = gj
                plan_scores(0)
                t_pv_last = None
                t_ls_last = None
                pending_pv = [None]
                def drain_pv():
                    if pending_pv[0] is not None:
                        pending_pv[0]()
                        pending_pv[0] = None
                for j in range(G):
                    bs = sc_slot[j]
                    bp = (gj0 + j) % 4
                    kb0, kb1 = 2 * j, 2 * j + 1
                    if j + 1 < G:
                        plan_scores(j + 1)
                    drain_pv()
                    we = [("t", sc_tgt[j][1])] + ([pT_free[bp]] if pT_free[bp] else [])
                    def exp_fn(e, bp=bp, bs=bs, h=h):
                        return e.activation(pT[bp][:], psSv[bs], AF.Exp,
                                            bias=m_bias[:, h:h + 1], scale=INV_SQRT_HD)
                    t_ex = A("scalar", exp_fn, waits=we, inc="a")
                    psS_free[bs] = ("a", t_ex[1])
                    dep = ("a", t_ex[1])
                    if 2 * j >= cnt - 8:
                        w0 = kb0 - (cnt - 8)
                        w1 = kb1 - (cnt - 8)
                        wm = [dep, t_msk]
                        def msk_fn(e, bp=bp, s=s, w0=w0, w1=w1):
                            m0 = masks_sb[:, (s * 8 + w0) * 512:(s * 8 + w0 + 1) * 512]
                            m1 = masks_sb[:, (s * 8 + w1) * 512:(s * 8 + w1 + 1) * 512]
                            e.tensor_tensor(pT[bp][:, 0:512], pT[bp][:, 0:512], m0, ALU.mult)
                            return e.tensor_tensor(pT[bp][:, 512:1024], pT[bp][:, 512:1024], m1, ALU.mult)
                        t_mk = A("vector", msk_fn, waits=wm, inc="v")
                        dep = ("v", t_mk[1])
                    def mk_pvls(dep=dep, bp=bp, kb0=kb0, kb1=kb1, j=j, cnt=cnt, bk=bk):
                        def emit_pvls():
                            nonlocal t_pv_last, t_ls_last
                            wpv = [dep]
                            if j == 0 and psPV_free[hs_cur % 2]:
                                wpv.append(psPV_free[hs_cur % 2])
                            def pv_fn(e, bp=bp, kb0=kb0, kb1=kb1, bk=bk, slot=pv_slot, j=j, cnt=cnt):
                                e.matmul(slot, vbuf[bk][:, kb0 * 128:(kb0 + 1) * 128],
                                         pT[bp][:, 0:512], start=(j == 0), stop=False)
                                return e.matmul(slot, vbuf[bk][:, kb1 * 128:(kb1 + 1) * 128],
                                                pT[bp][:, 512:1024], start=False,
                                                stop=(2 * j + 1 == cnt - 1))
                            t_pv_last = A("tensor", pv_fn, waits=wpv, inc="t")
                            wls = [dep]
                            if j == 0 and lsum_free2[hs_cur % 2]:
                                wls.append(lsum_free2[hs_cur % 2])
                            def ls_fn(e, bp=bp, j=j, cnt=cnt, pls=psLSv[hs_cur % 2]):
                                e.matmul(pls, ones_b[:, 0:1], pT[bp][:, 0:512],
                                         start=(j == 0), stop=False)
                                return e.matmul(pls, ones_b[:, 0:1], pT[bp][:, 512:1024],
                                                start=False, stop=(2 * j + 1 == cnt - 1))
                            t_ls_last = A("tensor", ls_fn, waits=wls, inc="t")
                            pT_free[bp] = ("t", t_ls_last[1])
                        return emit_pvls
                    pending_pv[0] = mk_pvls()
                    gj += 1
                drain_pv()
                def rec_fn(e, pls=psLSv[hs_cur % 2]):
                    e.reciprocal(strip[0:1, 1024:1536], pls)
                    return e.tensor_copy(stripb[0:1, 512:1024], strip[0:1, 1024:1536])
                t_rc = A("vector", rec_fn, waits=[("t", t_ls_last[1])], inc="v")
                wlb = [("v", t_rc[1])] + ([lbc_free] if lbc_free else [])
                t_lb = A("tensor", lambda e, plb=psLBv[hs_cur % 2]: e.matmul(
                             plb[:, 0:512], ones_b[0:1, :],
                             stripb[0:1, 512:1024], start=True, stop=True),
                         waits=wlb, inc="t")
                ba = hs_cur % 2
                wsc = [("t", t_lb[1]), ("t", t_pv_last[1])]
                if aout_free[ba]:
                    wsc.append(dnow())
                def scale_fn(e, ba=ba, slot=pv_slot, plb=psLBv[hs_cur % 2]):
                    e.tensor_copy(linv_bc[:], plb[:, 0:512])
                    return e.tensor_tensor(aout[ba][:], slot, linv_bc[:], ALU.mult)
                t_sc2 = A("vector", scale_fn, waits=wsc, inc="v")
                lbc_free = ("v", t_sc2[1])
                lsum_free2[hs_cur % 2] = ("v", t_sc2[1])
                psPV_free[hs_cur % 2] = ("v", t_sc2[1])
                t_as = dma("scalar", attn_sc[h * P:(h + 1) * P, s * 512:(s + 1) * 512],
                           aout[ba][:], waits=[("v", t_sc2[1])], q="dout")
                aout_free[ba] = ("dout", t_as[1])
                attn_written[(h, s)] = t_as[1]
                hs += 1
            qbuf_free[bq] = ("t", pl.cnt["t"])
        kbuf_free[bk] = ("t", pl.cnt["t"])
        vbuf_free[bk] = ("t", pl.cnt["t"])

    pl.barrier()

    c2_start_dout = dnow()
    wstgC_free = [None, None]
    ybuf2_free = [None, None]
    ysq2_free = [None, None]
    fbuf_free = [None, None]
    wbfC_free = [None, None]
    psO_free = [None, None]
    xt_free = [None, None]
    obuf_free = [None, None]
    psOv = [psA[:, 0:1024], psA[:, 1024:2048]]
    oidx = [0]
    for th in range(2):
        adin = []
        for dc in range(DC):
            w = []
            if th == 0 and dc == 0:
                w.append(c2_start_dout)
            if th == 1 and dc == 0:
                w.append(("t", pl.cnt["t"]))
            t_a = dma("sync", abuf[:, dc * 1024:(dc + 1) * 1024],
                      attn_sc[dc * P:(dc + 1) * P, th * 1024:(th + 1) * 1024], waits=w,
                      q="dab")
            adin.append(t_a)
        last_sq = None
        for oc in range(DC):
            by = oc % 2
            wy = [ybuf2_free[by]] if ybuf2_free[by] else []
            if th == 0 and oc == 0:
                wy.append(c2_start_dout)
            if th == 1 and oc == 0:
                wy.append(("v", pl.cnt["v"]))
            t_y = dma("sync", ybuf2[by][:],
                      y_sc[oc * P:(oc + 1) * P, th * 1024:(th + 1) * 1024], waits=wy,
                      q="dy%d" % by)
            wv_ = [t_y] + ([ysq2_free[by]] if ysq2_free[by] else [])
            t_sq2 = A("vector", lambda e, by=by: e.tensor_mul(ysq2[by][:], ybuf2[by][:],
                                                              ybuf2[by][:]), waits=wv_, inc="v")
            def ymm_fn(e, oc=oc, by=by):
                st, sp = oc == 0, oc == DC - 1
                e.matmul(psB[0:1, 0:512], ones_b[:, 0:1], ybuf2[by][:, 0:512], start=st, stop=sp)
                e.matmul(psB[0:1, 512:1024], ones_b[:, 0:1], ybuf2[by][:, 512:1024], start=st, stop=sp)
                e.matmul(psB[0:1, 1024:1536], ones_b[:, 0:1], ysq2[by][:, 0:512], start=st, stop=sp)
                return e.matmul(psB[0:1, 1536:2048], ones_b[:, 0:1], ysq2[by][:, 512:1024],
                                start=st, stop=sp)
            t_my = A("tensor", ymm_fn, waits=[("v", t_sq2[1])], inc="t")
            ybuf2_free[by] = ("t", t_my[1])
            ysq2_free[by] = ("t", t_my[1])
            last_sq = t_my
        def stripC_fn(e):
            e.tensor_scalar_mul(strip[0:1, 0:1024], psB[0:1, 0:1024], 1.0 / D)
            e.tensor_scalar_mul(strip[0:1, 1024:2048], psB[0:1, 1024:2048], 1.0 / D)
            e.tensor_mul(otmp[0:1, 0:1024], strip[0:1, 0:1024], strip[0:1, 0:1024])
            e.tensor_sub(strip[0:1, 1024:2048], strip[0:1, 1024:2048], otmp[0:1, 0:1024])
            return e.tensor_scalar_add(strip[0:1, 1024:2048], strip[0:1, 1024:2048], EPS)
        t_sp = A("vector", stripC_fn, waits=[("t", last_sq[1])], inc="v")
        t_ac = A("scalar", lambda e: e.activation(otmp[0:1, 0:1024], strip[0:1, 1024:2048],
                                                  AF.Sqrt), waits=[("v", t_sp[1])], inc="a")
        def strip2C_fn(e):
            e.reciprocal(strip[0:1, 1024:2048], otmp[0:1, 0:1024])
            e.tensor_mul(strip[0:1, 0:1024], strip[0:1, 0:1024], strip[0:1, 1024:2048])
            e.tensor_copy(stripb[0:1, 0:1024], strip[0:1, 1024:2048])
            return e.tensor_copy(ybuf2[0][0:1, 0:1024], strip[0:1, 0:1024])
        t_s2 = A("vector", strip2C_fn, waits=[("a", t_ac[1])], inc="v")
        def bcC_fn(e):
            e.matmul(psB[:, 0:512], ones_b[0:1, :], stripb[0:1, 0:512], start=True, stop=True)
            e.matmul(psB[:, 512:1024], ones_b[0:1, :], stripb[0:1, 512:1024], start=True, stop=True)
            e.matmul(psB[:, 1024:1536], ones_b[0:1, :], ybuf2[0][0:1, 0:512], start=True, stop=True)
            return e.matmul(psB[:, 1536:2048], ones_b[0:1, :], ybuf2[0][0:1, 512:1024],
                            start=True, stop=True)
        t_bc = A("tensor", bcC_fn, waits=[("v", t_s2[1])], inc="t")
        ybuf2_free[0] = ("t", t_bc[1])
        bc_ready = ("t", t_bc[1])
        for w2 in range(16):
            bsel = w2 % 2
            cast_tgts = [None, None]
            for half in range(2):
                t_w = dma("sync", wstgC[half][:].rearrange("p (c f) -> p c f", c=DC),
                          wo3[:, :, w2 * 256 + half * 128: w2 * 256 + (half + 1) * 128],
                          waits=[wstgC_free[half]], q="dw3")
                wc = [t_w] + ([wbfC_free[bsel]] if (wbfC_free[bsel] and half == 0) else [])
                dst3 = bass.AP(wbfC[bsel], wbfC[bsel][:].offset + half * 128,
                               [list(wbfC[bsel][:].ap[0]), [256, DC], [1, 128]])
                def castC_fn(e, half=half, dst3=dst3):
                    return e.tensor_copy(dst3, wstgC[half][:].rearrange("p (c f) -> p c f", c=DC))
                t_cast = A("vector", castC_fn, waits=wc, inc="v")
                wstgC_free[half] = t_cast
                cast_tgts[half] = t_cast
            for sub in range(2):
                oc = w2 * 2 + sub
                bo = oidx[0] % 2
                b2o = oidx[0] % 2
                oidx[0] += 1
                t_cast = cast_tgts[sub]
                wt = [("v", t_cast[1]), adin[-1]]
                if psO_free[bo]:
                    wt.append(psO_free[bo])
                def omm_fn(e, bsel=bsel, sub=sub, bo=bo):
                    for dc in range(DC):
                        e.matmul(psOv[bo][:, 0:512],
                                 wbfC[bsel][:, dc * 256 + sub * 128: dc * 256 + sub * 128 + 128],
                                 abuf[:, dc * 1024: dc * 1024 + 512],
                                 start=(dc == 0), stop=(dc == DC - 1))
                    for dc in range(DC):
                        r = e.matmul(psOv[bo][:, 512:1024],
                                     wbfC[bsel][:, dc * 256 + sub * 128: dc * 256 + sub * 128 + 128],
                                     abuf[:, dc * 1024 + 512: dc * 1024 + 1024],
                                     start=(dc == 0), stop=(dc == DC - 1))
                    return r
                t_mm = A("tensor", omm_fn, waits=wt, inc="t")
                if sub == 1:
                    wbfC_free[bsel] = ("t", t_mm[1])
                t_xl = dma("sync", xtile[b2o][:],
                           xown[oc * P:(oc + 1) * P, th * 1024:(th + 1) * 1024],
                           waits=[xt_free[b2o]], q="dxf%d" % b2o)
                by = oc % 2
                wy = [bc_ready] + ([ybuf2_free[by]] if ybuf2_free[by] else [])
                t_y2 = dma("sync", ybuf2[by][:],
                           y_sc[oc * P:(oc + 1) * P, th * 1024:(th + 1) * 1024], waits=wy,
                           q="dy%d" % by)
                wn = [t_y2] + ([fbuf_free[by]] if fbuf_free[by] else [])
                def norm_fn(e, oc=oc, by=by):
                    e.tensor_tensor(fbuf[by][:], ybuf2[by][:], psB[:, 0:1024], ALU.mult)
                    e.tensor_tensor(fbuf[by][:], fbuf[by][:], psB[:, 1024:2048], ALU.subtract)
                    return e.tensor_scalar(fbuf[by][:], fbuf[by][:], gam_col[:, oc:oc + 1],
                                           bet_col[:, oc:oc + 1], ALU.mult, ALU.add)
                t_nm = A("vector", norm_fn, waits=wn, inc="v")
                ybuf2_free[by] = ("v", t_nm[1])
                wv_ = [("t", t_mm[1]), t_xl, ("v", t_nm[1])]
                if obuf_free[b2o]:
                    wv_.append(dnow())
                def oadd_fn(e, bo=bo, b2o=b2o, by=by):
                    e.tensor_tensor(otmp[:], psOv[bo], xtile[b2o][:], ALU.add)
                    return e.tensor_tensor(obuf[b2o][:], otmp[:], fbuf[by][:], ALU.add)
                t_oa = A("vector", oadd_fn, waits=wv_, inc="v")
                fbuf_free[by] = ("v", t_oa[1])
                psO_free[bo] = ("v", t_oa[1])
                xt_free[b2o] = ("v", t_oa[1])
                t_os = dma("scalar", outf[oc * P:(oc + 1) * P, th * 1024:(th + 1) * 1024],
                           obuf[b2o][:], waits=[("v", t_oa[1])], q="dout")
                obuf_free[b2o] = ("dout", t_os[1])

    if dbg:
        pl.barrier()
        for nm, src, shp, dt in [
            ("d_xn", xn_sc, [D, 4096], BF16),
            ("d_k", k_sc, [NKV * P, 4096], BF16),
            ("d_v", v_sc, [4096, NKV * HD], BF16),
            ("d_q", q_sc, [32 * P, TOWN], BF16),
            ("d_y", y_sc, [D, TOWN], BF16),
            ("d_ffn", ffn_sc, [D, TOWN], F32),
            ("d_attn", attn_sc, [32 * P, TOWN], BF16),
        ]:
            dst = nc.dram_tensor(nm, shp, dt, kind="ExternalOutput").ap()
            dma("sync", dst, src, q="dc")
        d_st = nc.dram_tensor("d_st", [P, 72], F32, kind="ExternalOutput").ap()
        stt = sb("d_stt", [P, 72], F32)
        A("vector", lambda e: (e.tensor_copy(stt[:, 0:8], ksqmax[:]),
                               e.tensor_copy(stt[:, 8:40], qsqmax[:]),
                               e.tensor_copy(stt[:, 40:72], m_bias[:]))[-1], inc="v")
        dma("sync", d_st, stt[:], waits=[("v", pl.cnt["v"])], q="dc")
    final = pl.snap()

    import contextlib
    with contextlib.ExitStack() as stack:
        sems = {}
        for name in pl.cnt:
            sems[name] = stack.enter_context(nc.semaphore("s_" + name))
        block = stack.enter_context(nc.Block())
        own_of = {"tensor": "t", "vector": "v", "scalar": "a", "gpsimd": "p",
                  "sync": None}
        dma_sems = {n for n in pl.cnt if n not in ("t", "v", "a", "p")}

        def emit(eng_name):
            def go(eng):
                own = own_of[eng_name]
                for fn, waits, tgt in pl.ops[eng_name]:
                    for s, val in waits:
                        if s == own:
                            continue
                        eng.wait_ge(sems[s], val)
                    inst = fn(eng) if fn is not None else None
                    if tgt is not None:
                        s, val = tgt
                        if inst is None:
                            inst = eng.nop()
                        inst.then_inc(sems[s], 16 if s in dma_sems else 1)
                if eng_name == "sync":
                    eng.wait_ge(sems["dout"], final["dout"])
            return go

        block.tensor(emit("tensor"))
        block.vector(emit("vector"))
        block.scalar(emit("scalar"))
        block.gpsimd(emit("gpsimd"))
        block.sync(emit("sync"))

    return nc


def _host_tables(j):
    inv = 10000.0 ** (-(np.arange(64, dtype=np.float64) * 2.0 / 128.0))
    pos = np.arange(4096, dtype=np.float64)
    fr = pos[None, :] * inv[:, None]
    cos = np.cos(fr).astype(np.float32)
    sin = np.sin(fr).astype(np.float32)
    cosb = np.concatenate([cos, cos], 0)
    sinx = np.concatenate([-sin, sin], 0)
    own_cols = np.concatenate([np.arange(b * 512, (b + 1) * 512) for b in OWNBLK[j]])
    cosq = np.ascontiguousarray(cosb[:, own_cols])
    sinq = np.ascontiguousarray(sinx[:, own_cols])
    m = np.zeros((32, 128, 512), np.float32)
    for s in range(4):
        ob = OWNBLK[j][s]
        cnt = SLOT_CNT[s]
        qabs = ob * 512 + np.arange(512)
        for w in range(8):
            kb = cnt - 8 + w
            kabs = kb * 128 + np.arange(128)
            m[s * 8 + w] = (kabs[:, None] <= qabs[None, :]).astype(np.float32)
    bf = ml_dtypes.bfloat16
    return (cosb.astype(bf), sinx.astype(bf), cosq.astype(bf), sinq.astype(bf),
            np.ascontiguousarray(m.reshape(32 * 128, 512)).astype(bf), own_cols)


def kernel(L0_attn_x, L0_attn_Wq, L0_attn_Wk, L0_attn_Wv, L0_attn_Wo,
           L0_ffn_weight, L0_ffn_bias, L0_ffn_g, L0_ffn_h,
           L0_ffn_ln_gamma, L0_ffn_ln_beta, _dbg=False):
    x = np.ascontiguousarray(np.asarray(L0_attn_x, dtype=np.float32))
    wq_ = np.ascontiguousarray(np.asarray(L0_attn_Wq, dtype=np.float32))
    wk_ = np.ascontiguousarray(np.asarray(L0_attn_Wk, dtype=np.float32))
    wv_ = np.ascontiguousarray(np.asarray(L0_attn_Wv, dtype=np.float32))
    wo_ = np.ascontiguousarray(np.asarray(L0_attn_Wo, dtype=np.float32))
    wfT_ = np.ascontiguousarray(np.asarray(L0_ffn_weight, dtype=np.float32).T)

    def col(v):
        return np.ascontiguousarray(np.asarray(v, dtype=np.float32).reshape(DC, P).T)

    vecs_ = np.ascontiguousarray(np.concatenate(
        [col(L0_ffn_bias), col(L0_ffn_g), col(L0_ffn_h),
         col(L0_ffn_ln_gamma), col(L0_ffn_ln_beta),
         np.zeros((P, DC), np.float32)], axis=1))

    tabs = [_host_tables(0), _host_tables(1)]
    xT = [np.ascontiguousarray(x[b].T) for b in range(4)]

    nc = build_nc(dbg=_dbg)
    in_maps = []
    for c in range(8):
        b, j = c // 2, c % 2
        cosb_, sinx_, cosq_, sinq_, masks_, own_cols = tabs[j]
        xown_ = np.ascontiguousarray(xT[b][:, own_cols])
        in_maps.append({
            "xT": xT[b], "xown": xown_, "wq": wq_, "wk": wk_, "wv": wv_,
            "wo": wo_, "wfT": wfT_, "vecs": vecs_,
            "cosb": cosb_, "sinx": sinx_, "cosq": cosq_, "sinq": sinq_,
            "masks": masks_,
        })
    res = run_bass_kernel_spmd(nc, in_maps, core_ids=list(range(8)))
    out = np.empty((4, 4096, 4096), np.float32)
    for c in range(8):
        b, j = c // 2, c % 2
        of = np.asarray(res.results[c]["outf"])
        for s in range(4):
            ob = OWNBLK[j][s]
            out[b, ob * 512:(ob + 1) * 512, :] = of[:, s * 512:(s + 1) * 512].T
    if _dbg:
        return out, res
    return out


if __name__ == "__main__":
    import reference as _r
    inp = _r.setup_inputs()
    o = kernel(**{k: np.asarray(v) for k, v in inp.items()})
    e = np.asarray(_r.reference(**inp))
    err = o - e
    print("rvr", float((err ** 2).mean() / (np.asarray(e) ** 2).mean()),
          "max", float(np.abs(err).max()))


# version 12
# speedup vs baseline: 1.2372x; 1.0357x over previous
import math
import numpy as np
import ml_dtypes
import concourse.bass as bass
import concourse.mybir as mybir
from concourse.bass_utils import run_bass_kernel_spmd

F32 = mybir.dt.float32
BF16 = mybir.dt.bfloat16
AF = mybir.ActivationFunctionType
ALU = mybir.AluOpType
AX = mybir.AxisListType.X

D = 4096
DC = 32
P = 128
TOWN = 2048
NKV = 8
HD = 128
SLOT_CNT = (8, 16, 24, 32)
OWNBLK = ((0, 3, 4, 7), (1, 2, 5, 6))
INV_SQRT_HD = 1.0 / math.sqrt(HD)
EPS = 1e-5

ENGINES = ("tensor", "vector", "scalar", "gpsimd", "sync")


class Plan:

    def __init__(self):
        self.ops = {e: [] for e in ENGINES}
        self.cnt = {"t": 0, "v": 0, "a": 0, "p": 0, "dout": 0}

    def ensure(self, name):
        if name not in self.cnt:
            self.cnt[name] = 0

    def add(self, eng, fn, waits=(), inc=None, inc_by=1):
        for s, val in waits:
            self.ensure(s)
            assert val <= self.cnt[s], f"wait {s}>={val} planned only {self.cnt[s]}"
        tgt = None
        if inc is not None:
            self.ensure(inc)
            self.cnt[inc] += inc_by
            tgt = (inc, self.cnt[inc])
        self.ops[eng].append((fn, tuple((s, v) for s, v in waits if v > 0), tgt))
        return tgt

    def snap(self):
        return dict(self.cnt)

    def barrier(self):
        c = self.snap()
        for e in ENGINES:
            self.ops[e].append((None, tuple((s, c[s]) for s in c if c[s] > 0), None))


def build_nc(dbg=False):
    nc = bass.Bass("TRN2", target_bir_lowering=False, debug=False)
    dbg_outs = {}
    def dbg_dram(nm, shp, dt):
        ap = nc.dram_tensor(nm, shp, dt, kind="ExternalOutput").ap()
        dbg_outs[nm] = ap
        return ap

    xT = nc.dram_tensor("xT", [D, 4096], F32, kind="ExternalInput").ap()
    xown = nc.dram_tensor("xown", [D, TOWN], F32, kind="ExternalInput").ap()
    wq = nc.dram_tensor("wq", [D, D], F32, kind="ExternalInput").ap()
    wk = nc.dram_tensor("wk", [D, 1024], F32, kind="ExternalInput").ap()
    wv = nc.dram_tensor("wv", [D, 1024], F32, kind="ExternalInput").ap()
    wo = nc.dram_tensor("wo", [D, D], F32, kind="ExternalInput").ap()
    wfT = nc.dram_tensor("wfT", [D, D], F32, kind="ExternalInput").ap()
    vecs = nc.dram_tensor("vecs", [P, 6 * DC], F32, kind="ExternalInput").ap()
    cosb = nc.dram_tensor("cosb", [P, 4096], BF16, kind="ExternalInput").ap()
    sinx = nc.dram_tensor("sinx", [P, 4096], BF16, kind="ExternalInput").ap()
    cosq = nc.dram_tensor("cosq", [P, TOWN], BF16, kind="ExternalInput").ap()
    sinq = nc.dram_tensor("sinq", [P, TOWN], BF16, kind="ExternalInput").ap()
    masks = nc.dram_tensor("masks", [32 * P, 512], BF16, kind="ExternalInput").ap()
    outf = nc.dram_tensor("outf", [D, TOWN], F32, kind="ExternalOutput").ap()

    xn_sc = nc.dram_tensor("xn_sc", [D, 4096], BF16, kind="Internal").ap()
    k_sc = nc.dram_tensor("k_sc", [NKV * P, 4096], BF16, kind="Internal").ap()
    v_sc = nc.dram_tensor("v_sc", [4096, NKV * HD], BF16, kind="Internal").ap()
    q_sc = nc.dram_tensor("q_sc", [32 * P, TOWN], BF16, kind="Internal").ap()
    y_sc = nc.dram_tensor("y_sc", [D, TOWN], BF16, kind="Internal").ap()
    ffn_sc = nc.dram_tensor("ffn_sc", [D, TOWN], F32, kind="Internal").ap()
    attn_sc = nc.dram_tensor("attn_sc", [32 * P, TOWN], BF16, kind="Internal").ap()

    xT3 = xT.rearrange("(c p) t -> p c t", p=P)
    xown3 = xown.rearrange("(c p) t -> p c t", p=P)
    xn3 = xn_sc.rearrange("(c p) t -> p c t", p=P)
    wq3 = wq.rearrange("(c p) o -> p c o", p=P)
    wf3 = wfT.rearrange("(c p) o -> p c o", p=P)
    wo3 = wo.rearrange("(c p) o -> p c o", p=P)
    wk3 = wk.rearrange("(c p) o -> p c o", p=P)
    wv3 = wv.rearrange("(c p) o -> p c o", p=P)
    k3 = k_sc.rearrange("(h p) t -> p h t", p=P)
    v3 = v_sc.rearrange("(kb p) c -> p kb c", p=P)
    masks3 = masks.rearrange("(m p) c -> p m c", p=P)

    probe_a = nc.alloc_sbuf_tensor("arena_probe", [P, 32], mybir.dt.uint8)
    base0 = nc.lookup_mloc(probe_a).addr + 32
    ARENA = 229376 - base0 - 2048
    arena = nc.alloc_sbuf_tensor("arena", [P, ARENA], mybir.dt.uint8)
    base = nc.lookup_mloc(arena).addr
    assert base == base0, (base, base0)
    cursor = [base]

    def sb(name, shape, dtype):
        nbytes = shape[1] * (2 if dtype == BF16 else 4)
        at = cursor[0]
        cursor[0] = at + ((nbytes + 31) // 32) * 32
        assert cursor[0] <= base + ARENA, (name, cursor[0])
        return nc.alloc_sbuf_tensor_at(name, shape, dtype, offset=at)

    ones_b = sb("ones_b", [P, 128], BF16)
    ones_f = sb("ones_f", [P, 4], F32)
    vecs_sb = sb("vecs_sb", [P, 6 * DC], F32)
    g_col = sb("g_col", [P, DC], F32)
    h_col = sb("h_col", [P, DC], F32)
    bias_col = sb("bias_col", [P, DC], F32)
    gam_col = sb("gam_col", [P, DC], F32)
    bet_col = sb("bet_col", [P, DC], F32)
    bh_col = sb("bh_col", [P, DC], F32)
    ksqmax = sb("ksqmax", [P, 8], F32)
    qsqmax = sb("qsqmax", [P, 32], F32)
    norms = sb("norms", [P, 256], F32)
    m_bias = sb("m_bias", [P, 32], F32)
    strip = sb("strip", [P, 2048], F32)
    stripb = sb("stripb", [P, 1024], BF16)
    vstage = sb("vstage", [P, 1024], F32)
    PERSIST_END = cursor[0]
    assert PERSIST_END < 44 * 1024, PERSIST_END

    wk_sb = sb("wk_sb", [P, DC * 1024], BF16)
    wv_sb = sb("wv_sb", [P, DC * 1024], BF16)
    A1_DYN = cursor[0]
    xst = [sb(f"xst{i}", [P, D], F32) for i in range(2)]
    xbA = [sb(f"xbA{i}", [P, D], BF16) for i in range(2)]
    xnt = sb("xnt", [P, D], BF16)
    xsqA = xnt
    abc_sb = sb("abc_sb", [P, 128], BF16)
    bbc_sb = sb("bbc_sb", [P, 128], BF16)
    A1A_END = cursor[0]
    cursor[0] = A1_DYN
    cos_sb = sb("cos_sb", [P, 4096], BF16)
    sin_sb = sb("sin_sb", [P, 4096], BF16)
    xng = [sb("xng0", [P, DC * 256], BF16)] * 2
    kf = [sb(f"kf{i}", [P, 256], BF16) for i in range(2)]
    kst = [sb(f"kst{i}", [P, NKV * 256], BF16) for i in range(2)]
    ksqt = [sb(f"ksqt{i}", [P, 256], BF16) for i in range(2)]
    vbb = [sb(f"vbb{i}", [P, 1024], BF16) for i in range(2)]
    t1p = sb("t1p", [P, 256], F32)
    t2p = sb("t2p", [P, 256], F32)
    A1B_END = cursor[0]
    assert max(A1A_END, A1B_END) <= 229376, (A1A_END, A1B_END)

    cursor[0] = PERSIST_END
    xn_own = sb("xn_own", [P, DC * 1024], BF16)
    wstg = [sb(f"wstg{i}", [P, DC * 128], F32) for i in range(2)]
    wbf = [sb(f"wbf{i}", [P, DC * 256], BF16) for i in range(2)]
    cosq_sb = sb("cosq_sb", [P, TOWN], BF16)
    sinq_sb = sb("sinq_sb", [P, TOWN], BF16)
    qf_sb = sb("qf_sb", [P, 1024], BF16)
    qrot = [sb(f"qrot{i}", [P, 1024], BF16) for i in range(2)]
    qsq_sb = sb("qsq_sb", [P, 1024], BF16)
    yb = [sb(f"yb{i}", [P, 1024], BF16) for i in range(2)]
    xstA2 = sb("xstA2", [P, D], F32)
    xbA2 = sb("xbA2", [P, D], BF16)
    xsqA2 = sb("xsqA2", [P, D], BF16)
    t1q = sb("t1q", [P, 1024], F32)
    t2q = sb("t2q", [P, 1024], F32)
    A2_END = cursor[0]
    assert A2_END <= 229376, A2_END

    cursor[0] = PERSIST_END
    masks_sb = sb("masks_sb", [P, 32 * 512], BF16)
    kbuf = [sb(f"kbuf{i}", [P, 4096], BF16) for i in range(2)]
    vbuf = [sb(f"vbuf{i}", [P, 4096], BF16) for i in range(2)]
    qbuf = [sb(f"qbuf{i}", [P, TOWN], BF16) for i in range(2)]
    pT = [sb(f"pT{i}", [P, 1024], BF16) for i in range(4)]
    lacc = [sb(f"lacc{i}", [P, 512], F32) for i in range(2)]
    linv_bc = sb("linv_bc", [P, 512], F32)
    aout = [sb(f"aout{i}", [P, 512], BF16) for i in range(2)]
    B_END = cursor[0]
    assert B_END <= 229376, B_END

    cursor[0] = PERSIST_END
    abuf = sb("abuf", [P, DC * 1024], BF16)
    wstgC = [sb(f"wstgC{i}", [P, DC * 128], F32) for i in range(2)]
    wbfC = [sb(f"wbfC{i}", [P, DC * 256], BF16) for i in range(2)]
    xtile = [sb(f"xtile{i}", [P, 1024], F32) for i in range(2)]
    otmp = sb("otmp", [P, 1024], F32)
    obuf = [sb(f"obuf{i}", [P, 1024], F32) for i in range(2)]
    ybuf2 = [sb(f"ybuf2{i}", [P, 1024], BF16) for i in range(2)]
    ysq2 = [sb(f"ysq2{i}", [P, 1024], BF16) for i in range(2)]
    fbuf = [sb(f"fbuf{i}", [P, 1024], F32) for i in range(2)]
    C2_END = cursor[0]
    assert C2_END <= 229376, C2_END

    psA = nc.alloc_psum_tensor("psA", [P, 2048], F32)
    psB = nc.alloc_psum_tensor("psB", [P, 2048], F32)

    pl = Plan()
    A = pl.add

    def dma(eng, out_ap, in_ap, waits=(), q="dout"):
        def fn(e, out_ap=out_ap, in_ap=in_ap):
            return e.dma_start(out=out_ap, in_=in_ap)
        return A(eng, fn, waits=[w for w in waits if w], inc=q, inc_by=16)

    def dnow():
        return ("dout", pl.cnt.get("dout", 0))

    def bcast_over(handle, ap2, reps):
        return bass.AP(handle, ap2.offset,
                       [list(ap2.ap[0]), [0, reps], list(ap2.ap[1])])

    def chunk_bcast(handle, ap2, inner):
        return bass.AP(handle, ap2.offset,
                       [list(ap2.ap[0]), list(ap2.ap[1]), [0, inner]])

    t_vec = dma("sync", vecs_sb[:], vecs, q="dc")
    A("vector", lambda e: e.memset(ones_b[:], 1.0))
    A("vector", lambda e: e.memset(ones_f[:], 1.0))
    A("vector", lambda e: e.memset(ksqmax[:], 0.0))
    A("vector", lambda e: e.memset(qsqmax[:], 0.0))
    A("vector", lambda e: e.tensor_copy(bias_col[:], vecs_sb[:, 0:DC]), waits=[t_vec])
    A("vector", lambda e: e.tensor_copy(g_col[:], vecs_sb[:, DC:2 * DC]))
    A("vector", lambda e: e.tensor_copy(h_col[:], vecs_sb[:, 2 * DC:3 * DC]))
    A("vector", lambda e: e.tensor_copy(gam_col[:], vecs_sb[:, 3 * DC:4 * DC]))
    A("vector", lambda e: e.tensor_copy(bet_col[:], vecs_sb[:, 4 * DC:5 * DC]))
    A("vector", lambda e: e.tensor_mul(bh_col[:], bias_col[:], h_col[:]), inc="v")

    xb_free = [None, None]
    xnt_free = [None, None]
    stats_free = None
    bca_free = None
    vstage_free = [None]
    wkv_cast_last = None
    xn_written = {}

    wkv_iter = [("wk", dc) for dc in range(DC)] + [("wv", dc) for dc in range(DC)]
    wkv_pos = 0

    def plan_wkv(n):
        nonlocal wkv_pos, wkv_cast_last
        for _ in range(n):
            if wkv_pos >= len(wkv_iter):
                return
            which, dc = wkv_iter[wkv_pos]
            src = (wk3 if which == "wk" else wv3)[:, dc:dc + 1, :]
            dst = vstage[:].unsqueeze(1)
            tgt = dma("sync", dst, src, waits=[vstage_free[0]], q="dw")
            wsb = wk_sb if which == "wk" else wv_sb
            def cast_fn(e, wsb=wsb, dc=dc):
                return e.tensor_copy(wsb[:, dc * 1024:(dc + 1) * 1024], vstage[:])
            c = A("vector", cast_fn, waits=[tgt], inc="v")
            vstage_free[0] = c
            wkv_cast_last = c
            wkv_pos += 1

    def plan_ln_tile(src3, tcols, xst_, xb_, xsq_, stats_ps, bc_ps, pool_dst_fn,
                     war_load, war_cast, war_pool, ldq):
        nonlocal_waits = [w for w in war_load if w]
        t_x = dma("sync", xst_[:].rearrange("p (c t) -> p c t", c=DC),
                  src3[:, :, tcols], waits=nonlocal_waits, q=ldq)
        wcast = [t_x] + [w for w in war_cast if w]
        A("vector", lambda e: e.tensor_copy(xb_[:], xst_[:]), waits=wcast)
        t_xb = A("vector", lambda e: e.tensor_mul(xsq_[:], xb_[:], xb_[:]), inc="v")
        wt = [("v", t_xb[1]), stats_free] if stats_free else [("v", t_xb[1])]
        def stats_fn(e):
            for dc in range(DC):
                e.matmul(stats_ps[0:1, 0:128], ones_b[:, 0:1],
                         xb_[:, dc * 128:(dc + 1) * 128], start=(dc == 0), stop=(dc == DC - 1))
            for dc in range(DC):
                r = e.matmul(stats_ps[0:1, 128:256], ones_b[:, 0:1],
                             xsq_[:, dc * 128:(dc + 1) * 128], start=(dc == 0), stop=(dc == DC - 1))
            return r
        t_st = A("tensor", stats_fn, waits=wt, inc="t")
        def strip_fn(e):
            e.tensor_scalar_mul(strip[0:1, 0:128], stats_ps[0:1, 0:128], 1.0 / D)
            e.tensor_scalar_mul(strip[0:1, 128:256], stats_ps[0:1, 128:256], 1.0 / D)
            e.tensor_mul(strip[0:1, 256:384], strip[0:1, 0:128], strip[0:1, 0:128])
            e.tensor_sub(strip[0:1, 256:384], strip[0:1, 128:256], strip[0:1, 256:384])
            return e.tensor_scalar_add(strip[0:1, 256:384], strip[0:1, 256:384], EPS)
        t_sp = A("vector", strip_fn, waits=[("t", t_st[1])], inc="v")
        def act_fn(e):
            return e.activation(strip[0:1, 640:768], strip[0:1, 256:384], AF.Sqrt)
        t_ac = A("scalar", act_fn, waits=[("v", t_sp[1])], inc="a")
        def strip2_fn(e):
            e.reciprocal(strip[0:1, 384:512], strip[0:1, 640:768])
            e.tensor_mul(strip[0:1, 512:640], strip[0:1, 0:128], strip[0:1, 384:512])
            e.tensor_copy(stripb[0:1, 0:128], strip[0:1, 384:512])
            return e.tensor_copy(stripb[0:1, 128:256], strip[0:1, 512:640])
        t_s2 = A("vector", strip2_fn, waits=[("a", t_ac[1])], inc="v")
        wt2 = [("v", t_s2[1])] + ([bca_free] if bca_free else [])
        def bc_fn(e):
            e.matmul(bc_ps[:, 0:128], ones_b[0:1, :], stripb[0:1, 0:128], start=True, stop=True)
            return e.matmul(bc_ps[:, 128:256], ones_b[0:1, :], stripb[0:1, 128:256], start=True, stop=True)
        t_bc = A("tensor", bc_fn, waits=wt2, inc="t")
        def cpbc_fn(e):
            e.tensor_copy(abc_sb2[:], bc_ps[:, 0:128])
            return e.tensor_copy(bbc_sb2[:], bc_ps[:, 128:256])
        t_cp = A("vector", cpbc_fn, waits=[("t", t_bc[1])], inc="v")
        wp = [("v", t_cp[1])] + [w for w in war_pool if w]
        def pool_fn(e):
            a_b = bcast_over(abc_sb2, abc_sb2[:], DC)
            b_b = bcast_over(bbc_sb2, bbc_sb2[:], DC)
            return pool_dst_fn(e, a_b, b_b)
        t_pl = A("vector", pool_fn, waits=wp, inc="v")
        return t_pl, t_xb, ("v", t_sp[1]), ("v", t_pl[1])

    abc_sb2 = abc_sb
    bbc_sb2 = bbc_sb

    plan_wkv(8)
    sq_prev = None
    xnt_free1 = None
    cast_prev = [None, None]
    for i in range(32):
        b = i % 2
        def mk_pool_dst(b=b):
            def f(e, a_b, b_b):
                src = xbA[b][:].rearrange("p (c t) -> p c t", c=DC)
                dst = xnt[:].rearrange("p (c t) -> p c t", c=DC)
                e.tensor_tensor(dst, src, a_b, ALU.mult)
                return e.tensor_tensor(dst, dst, b_b, ALU.subtract)
            return f
        t_pl, sq_prev, stats_free, bca_free = plan_ln_tile(
            xT3, slice(i * 128, (i + 1) * 128), xst[b], xbA[b], xsqA,
            psA[:, 0:512], psA[:, 512:1024], mk_pool_dst(),
            war_load=[cast_prev[b]], war_cast=[xb_free[b], xnt_free1],
            war_pool=[xnt_free1], ldq="dx")
        cast_prev[b] = sq_prev
        xb_free[b] = bca_free
        t_wr = dma("scalar", xn3[:, :, i * 128:(i + 1) * 128],
                   xnt[:].rearrange("p (c t) -> p c t", c=DC),
                   waits=[bca_free], q="dout")
        xnt_free1 = ("dout", t_wr[1])
        xn_written[i] = t_wr[1]
        if dbg and i == 31:
            d1 = dbg_dram("d_xsq31", [P, D], BF16)
            d2 = dbg_dram("d_strip31", [1, 1024], F32)
            d3 = dbg_dram("d_xb31", [P, D], BF16)
            d4 = dbg_dram("d_abc31", [P, 128], BF16)
            d5 = dbg_dram("d_bbc31", [P, 128], BF16)
            dma("sync", d1, xsqA[:], waits=[stats_free], q="dc")
            dma("sync", d2, strip[0:1, 0:1024], waits=[stats_free, bca_free], q="dc")
            dma("sync", d3, xbA[i % 2][:], waits=[stats_free], q="dc")
            dma("sync", d4, abc_sb[:], waits=[bca_free])
            dma("sync", d5, bbc_sb[:], waits=[bca_free])
        plan_wkv(2)
    plan_wkv(len(wkv_iter) - wkv_pos)

    pl.barrier()

    t_cos = dma("sync", cos_sb[:], cosb, q="dc")
    t_sin = dma("sync", sin_sb[:], sinx, q="dc")
    GTOK = 256
    NG = 4096 // GTOK
    psK_free = [None, None]
    psV_free = [None, None]
    xng_free = [None, None]
    kf_free = [None, None]
    kst_free = [None, None]
    ksqt_free = [None, None]
    vbb_free = [None, None]
    kv_written = []
    psKv = [psA[:, 1024:1280], psA[:, 1536:1792]]
    psKstrip = [psA[0:1, 1024:1280], psA[0:1, 1536:1792]]
    psVv = [psB[:, 0:1024], psB[:, 1024:2048]]

    for g in range(NG):
        gc = slice(g * GTOK, (g + 1) * GTOK)
        bg = g % 2
        w = [("dout", xn_written[(g * GTOK + GTOK) // 128 - 1]), xng_free[0]]
        if g == 0:
            w += [t_sin, wkv_cast_last]
        t_xn = dma("sync", xng[bg][:].rearrange("p (c t) -> p c t", c=DC),
                   xn3[:, :, gc], waits=w, q="dg")
        last_rp = None
        pending_kq = [None]
        def drain_kq():
            if pending_kq[0] is not None:
                pending_kq[0]()
                pending_kq[0] = None
        for kh in range(NKV):
            bk = (g * NKV + kh) % 2
            wt = [t_xn] + ([psK_free[bk]] if psK_free[bk] else [])
            def kmm_fn(e, bk=bk, kh=kh, bg=bg):
                for dc in range(DC):
                    r = e.matmul(psKv[bk],
                                 wk_sb[:, dc * 1024 + kh * 128: dc * 1024 + (kh + 1) * 128],
                                 xng[bg][:, dc * GTOK:(dc + 1) * GTOK],
                                 start=(dc == 0), stop=(dc == DC - 1))
                return r
            t_k = A("tensor", kmm_fn, waits=wt, inc="t")
            drain_kq()
            wv_ = [("t", t_k[1])]
            if kh == 0 and kst_free[bg]:
                wv_.append(dnow())
            def rope_fn(e, bk=bk, kh=kh, bg=bg, gc=gc):
                dst = kst[bg][:, kh * 256:(kh + 1) * 256]
                e.tensor_tensor(t1p[:], psKv[bk], cos_sb[:, gc], ALU.mult)
                e.tensor_tensor(t2p[0:64, :], psKv[bk][64:128, :], sin_sb[0:64, gc], ALU.mult)
                e.tensor_tensor(t2p[64:128, :], psKv[bk][0:64, :], sin_sb[64:128, gc], ALU.mult)
                return e.tensor_tensor(dst, t1p[:], t2p[:], ALU.add)
            t_rp = A("vector", rope_fn, waits=wv_, inc="v")
            wp = [("v", t_rp[1])] + ([ksqt_free[bk]] if ksqt_free[bk] else [])
            def ksq_fn(e, bk=bk, kh=kh, bg=bg):
                dst = kst[bg][:, kh * 256:(kh + 1) * 256]
                return e.tensor_tensor(ksqt[bk][:], dst, dst, ALU.mult)
            t_sq3 = A("gpsimd", ksq_fn, waits=wp, inc="p")
            def deferred(bk=bk, kh=kh, g=g, t_sq3=t_sq3, t_rp=t_rp):
                t_kq = A("tensor", lambda e, bk=bk: e.matmul(psKstrip[bk], ones_b[:, 0:1],
                                                             ksqt[bk][:], start=True, stop=True),
                         waits=[("p", t_sq3[1]), ("v", t_rp[1])], inc="t")
                def kmax_fn(e, bk=bk, kh=kh, g=g):
                    return e.reduce_max(norms[0:1, kh * 16 + g: kh * 16 + g + 1],
                                        psKstrip[bk], axis=AX)
                t_km = A("vector", kmax_fn, waits=[("t", t_kq[1])], inc="v")
                psK_free[bk] = ("v", t_km[1])
                ksqt_free[bk] = ("v", t_km[1])
            pending_kq[0] = deferred
            last_rp = t_rp
        if dbg and g == 0:
            d6 = dbg_dram("d_wksb", [P, DC * 1024], BF16)
            dma("sync", d6, wk_sb[:], waits=[wkv_cast_last], q="dc")
            d7 = dbg_dram("d_kst0", [P, NKV * 256], BF16)
            dma("sync", d7, kst[bg][:], waits=[("v", last_rp[1])], q="dc")
        drain_kq()
        t_ks = dma("scalar", k3[:, :, gc], kst[bg][:].rearrange("p (h t) -> p h t", h=NKV),
                   waits=[("v", last_rp[1])], q="dout")
        kst_free[bg] = ("dout", t_ks[1])
        kv_written.append(t_ks[1])
        for th in range(2):
            bvp = (g * 2 + th) % 2
            wt = [t_xn] + ([psV_free[bvp]] if psV_free[bvp] else [])
            def vmm_fn(e, bvp=bvp, th=th, bg=bg):
                for dc in range(DC):
                    e.matmul(psVv[bvp][:, 0:512],
                             xng[bg][:, dc * GTOK + th * 128: dc * GTOK + th * 128 + 128],
                             wv_sb[:, dc * 1024: dc * 1024 + 512],
                             start=(dc == 0), stop=(dc == DC - 1))
                for dc in range(DC):
                    r = e.matmul(psVv[bvp][:, 512:1024],
                                 xng[bg][:, dc * GTOK + th * 128: dc * GTOK + th * 128 + 128],
                                 wv_sb[:, dc * 1024 + 512: dc * 1024 + 1024],
                                 start=(dc == 0), stop=(dc == DC - 1))
                return r
            t_v = A("tensor", vmm_fn, waits=wt, inc="t")
            wv_ = [("t", t_v[1])] + ([dnow()] if vbb_free[bvp] else [])
            t_vc = A("vector", lambda e, bvp=bvp: e.tensor_copy(vbb[bvp][:], psVv[bvp]),
                     waits=wv_, inc="v")
            psV_free[bvp] = ("v", t_vc[1])
            t_vs = dma("scalar", v_sc[(g * GTOK + th * 128):(g * GTOK + th * 128 + 128), :],
                       vbb[bvp][:], waits=[("v", t_vc[1])], q="dout")
            vbb_free[bvp] = ("dout", t_vs[1])
            kv_written.append(t_vs[1])
        xng_free[0] = ("t", pl.cnt["t"])

    pl.barrier()

    t_cq = dma("sync", cosq_sb[:], cosq, q="dc")
    t_sq = dma("sync", sinq_sb[:], sinq, q="dc")
    xbA2_free = None
    wstg_free = [None, None]
    wbf_free = [None, None]
    psQ_free = [None, None, None]
    qsq_free = None
    qidx = [0]
    qrot_free = [None, None]
    qrot_free2 = [None, None]
    yb_free = [None, None]
    q_written = {}
    y_written = {}
    psQv = [psA[:, 0:1024], psA[:, 1024:2048], psB[:, 1024:2048]]
    abc_sb2 = abc_sb
    bbc_sb2 = bbc_sb
    stats_free = None
    bca_free = None

    sqA2_prev = None
    for th in range(2):
        xn_ready = None
        xn_half = [None, None]
        for j in range(8):
            def mk_pool_dst(j=j):
                def f(e, a_b, b_b):
                    dst = bass.AP(xn_own, xn_own[:].offset + j * 128,
                                  [list(xn_own[:].ap[0]), [1024, DC], [1, 128]])
                    src = xbA2[:].rearrange("p (c t) -> p c t", c=DC)
                    e.tensor_tensor(dst, src, a_b, ALU.mult)
                    return e.tensor_tensor(dst, dst, b_b, ALU.subtract)
                return f
            war_pool = [xbA2_free]
            if th == 1 and j == 0:
                war_pool.append(("t", pl.cnt["t"]))
            t_pl, sqA2_prev, stats_free, bca_free = plan_ln_tile(
                xown3, slice(th * 1024 + j * 128, th * 1024 + (j + 1) * 128),
                xstA2, xbA2, xsqA2,
                psB[:, 0:512], psB[:, 512:1024], mk_pool_dst(),
                war_load=[sqA2_prev], war_cast=[xbA2_free], war_pool=war_pool, ldq="dx2")
            xbA2_free = bca_free
            xn_ready = bca_free
            if j == 3:
                xn_half[0] = bca_free
            if j == 7:
                xn_half[1] = bca_free
        pending_qn = [None]
        def drain_qn():
            if pending_qn[0] is not None:
                pending_qn[0]()
                pending_qn[0] = None
        for w2 in range(32):
            is_q = w2 < 16
            duo = w2 if is_q else w2 - 16
            bsel = w2 % 2
            cast_tgts = [None, None]
            for half in range(2):
                src3 = (wq3 if is_q else wf3)[:, :, duo * 256 + half * 128: duo * 256 + (half + 1) * 128]
                t_w = dma("sync", wstg[half][:].rearrange("p (c f) -> p c f", c=DC), src3,
                          waits=[wstg_free[half]], q="dw2%d" % half)
                wc = [t_w] + ([wbf_free[bsel]] if (wbf_free[bsel] and half == 0) else [])
                dst3 = bass.AP(wbf[bsel], wbf[bsel][:].offset + half * 128,
                               [list(wbf[bsel][:].ap[0]), [256, DC], [1, 128]])
                if is_q:
                    def cast_fn(e, bsel=bsel, half=half, dst3=dst3):
                        return e.tensor_copy(dst3, wstg[half][:].rearrange("p (c f) -> p c f", c=DC))
                    t_cast = A("vector", cast_fn, waits=wc, inc="v")
                    wstg_free[half] = t_cast
                else:
                    def sgn_fn(e, bsel=bsel, half=half, dst3=dst3):
                        return e.activation(dst3, wstg[half][:].rearrange("p (c f) -> p c f", c=DC),
                                            AF.Sign)
                    t_sgn = A("scalar", sgn_fn, waits=wc, inc="a")
                    wstg_free[half] = t_sgn
                    def gmul_fn(e, bsel=bsel, dst3=dst3):
                        g_b = chunk_bcast(g_col, g_col[:], 128)
                        return e.tensor_tensor(dst3, dst3, g_b, ALU.mult)
                    t_cast = A("vector", gmul_fn, waits=[t_sgn], inc="v")
                cast_tgts[half] = t_cast
            for sub in range(2):
                oc = duo * 2 + sub
                bq = qidx[0] % 3
                b2 = qidx[0] % 2
                qidx[0] += 1
                t_cast = cast_tgts[sub]
                wt = [("v", t_cast[1]), xn_half[0]] + ([psQ_free[bq]] if psQ_free[bq] else [])
                def wmm_fn(e, bsel=bsel, sub=sub, bq=bq):
                    for dc in range(DC):
                        r = e.matmul(psQv[bq][:, 0:512],
                                     wbf[bsel][:, dc * 256 + sub * 128: dc * 256 + sub * 128 + 128],
                                     xn_own[:, dc * 1024: dc * 1024 + 512],
                                     start=(dc == 0), stop=(dc == DC - 1))
                    return r
                t_mma = A("tensor", wmm_fn, waits=wt, inc="t")
                def wmm_fn2(e, bsel=bsel, sub=sub, bq=bq):
                    for dc in range(DC):
                        r = e.matmul(psQv[bq][:, 512:1024],
                                     wbf[bsel][:, dc * 256 + sub * 128: dc * 256 + sub * 128 + 128],
                                     xn_own[:, dc * 1024 + 512: dc * 1024 + 1024],
                                     start=(dc == 0), stop=(dc == DC - 1))
                    return r
                t_mm = A("tensor", wmm_fn2, waits=[xn_half[1]], inc="t")
                drain_qn()
                if sub == 1:
                    wbf_free[bsel] = ("t", t_mm[1])
                if is_q:
                    h = oc
                    wv_ = [("t", t_mm[1]), t_cq, t_sq]
                    if qrot_free[b2]:
                        wv_.append(dnow())
                    if qrot_free2[b2]:
                        wv_.append(qrot_free2[b2])
                    def qrope_fn(e, bq=bq, b2=b2, th=th):
                        cs = slice(th * 1024, (th + 1) * 1024)
                        e.tensor_tensor(t1q[:], psQv[bq], cosq_sb[:, cs], ALU.mult)
                        e.tensor_tensor(t2q[0:64, :], psQv[bq][64:128, :], sinq_sb[0:64, cs], ALU.mult)
                        e.tensor_tensor(t2q[64:128, :], psQv[bq][0:64, :], sinq_sb[64:128, cs], ALU.mult)
                        return e.tensor_tensor(qrot[b2][:], t1q[:], t2q[:], ALU.add)
                    t_rv = A("vector", qrope_fn, waits=wv_, inc="v")
                    wp = [("v", t_rv[1])] + ([qsq_free] if qsq_free else [])
                    def qsq_fn(e, b2=b2):
                        return e.tensor_tensor(qsq_sb[:], qrot[b2][:], qrot[b2][:], ALU.mult)
                    t_rp = A("gpsimd", qsq_fn, waits=wp, inc="p")
                    def deferred_qn(bq=bq, h=h, th=th, t_rp=t_rp, t_rv=t_rv):
                        def qn_fn(e, bq=bq):
                            e.matmul(psQv[bq][0:1, 0:512], ones_b[:, 0:1], qsq_sb[:, 0:512],
                                     start=True, stop=True)
                            return e.matmul(psQv[bq][0:1, 512:1024], ones_b[:, 0:1],
                                            qsq_sb[:, 512:1024], start=True, stop=True)
                        t_qn = A("tensor", qn_fn, waits=[("p", t_rp[1]), ("v", t_rv[1])], inc="t")
                        def qmax_fn(e, bq=bq, h=h, th=th):
                            e.tensor_copy(strip[0:1, 1024:1536], psQv[bq][0:1, 0:512])
                            e.tensor_copy(strip[0:1, 1536:2048], psQv[bq][0:1, 512:1024])
                            return e.reduce_max(norms[0:1, 128 + h * 2 + th: 129 + h * 2 + th],
                                                strip[0:1, 1024:2048], axis=AX)
                        t_qm = A("vector", qmax_fn, waits=[("t", t_qn[1])], inc="v")
                        psQ_free[bq] = ("v", t_qm[1])
                        globals()
                        nonlocal qsq_free
                        qsq_free = ("t", t_qn[1])
                    pending_qn[0] = deferred_qn
                    t_qs = dma("scalar", q_sc[h * P:(h + 1) * P, th * 1024:(th + 1) * 1024],
                               qrot[b2][:], waits=[("v", t_rv[1])], q="dout")
                    qrot_free[b2] = ("dout", t_qs[1])
                    qrot_free2[b2] = ("p", t_rp[1])
                    q_written[(h, th)] = t_qs[1]
                else:
                    o = oc
                    wv_ = [("t", t_mm[1])] + ([dnow()] if yb_free[b2] else [])
                    def yevac_fn(e, bq=bq, b2=b2, o=o):
                        return e.tensor_scalar(yb[b2][:], psQv[bq], h_col[:, o:o + 1],
                                               bh_col[:, o:o + 1], ALU.mult, ALU.add)
                    t_ye = A("vector", yevac_fn, waits=wv_, inc="v")
                    psQ_free[bq] = ("v", t_ye[1])
                    t_ys = dma("scalar", y_sc[o * P:(o + 1) * P, th * 1024:(th + 1) * 1024],
                               yb[b2][:], waits=[("v", t_ye[1])], q="dout")
                    yb_free[b2] = ("dout", t_ys[1])
                    y_written[(o, th)] = t_ys[1]

        drain_qn()

    pl.barrier()

    pl.barrier()

    b_start_dout = dnow()
    t_msk = dma("sync", masks_sb[:].rearrange("p (m c) -> p m c", m=32), masks3, q="dc")
    def mprep_fn(e):
        for kh in range(8):
            e.reduce_max(ksqmax[0:1, kh:kh + 1], norms[0:1, kh * 16:(kh + 1) * 16], axis=AX)
        ap0 = bass.AP(norms, norms[:].offset + 128, [list(norms[:].ap[0]), [2, 32]])
        ap1 = bass.AP(norms, norms[:].offset + 129, [list(norms[:].ap[0]), [2, 32]])
        return e.tensor_tensor(qsqmax[0:1, 0:32], ap0[0:1], ap1[0:1], ALU.max)
    t_mp = A("vector", mprep_fn, waits=[], inc="v")
    def mmul_fn(e):
        r = None
        for kh in range(8):
            kb = bass.AP(ksqmax, ksqmax[:].offset + kh, [list(ksqmax[:].ap[0]), [0, 4]])
            r = e.tensor_tensor(strip[0:1, kh * 4:(kh + 1) * 4],
                                qsqmax[0:1, kh * 4:(kh + 1) * 4], kb[0:1], ALU.mult)
        return r
    t_mm2 = A("gpsimd", mmul_fn, waits=[("v", t_mp[1])], inc="p")
    def mact_fn(e):
        return e.activation(strip[0:1, 32:64], strip[0:1, 0:32], AF.Sqrt, scale=1.0 / HD)
    t_ma = A("scalar", mact_fn, waits=[("p", t_mm2[1])], inc="a")
    t_mn = A("vector", lambda e: e.tensor_copy(stripb[0:1, 0:32], strip[0:1, 32:64]),
             waits=[("a", t_ma[1])], inc="v")
    t_mb = A("tensor", lambda e: e.matmul(psB[:, 0:32], ones_b[0:1, :], stripb[0:1, 0:32],
                                          start=True, stop=True),
             waits=[("v", t_mn[1])], inc="t")
    t_mc = A("vector", lambda e: e.tensor_scalar_mul(m_bias[:], psB[:, 0:32], -1.0),
             waits=[("t", t_mb[1])], inc="v")

    kbuf_free = [None, None]
    vbuf_free = [None, None]
    qbuf_free = [None, None]
    pT_free = [None, None, None, None]
    lacc_free = [None, None]
    aout_free = [None, None]
    psS_free = [None, None]
    psPV_free = [None, None]
    lsum_free = None
    lbc_free = ("v", t_mc[1])

    psSv = [psA[:, 0:1024], psA[:, 1024:2048]]
    psPVv = [psB[:, 0:512], psB[:, 512:1024]]
    psLSv = [psB[0:1, 1024:1536], psB[0:1, 1536:2048]]
    psLBv = [psB[:, 1024:1536], psB[:, 1536:2048]]
    lsum_free2 = [None, None]
    attn_written = {}
    kv_done = b_start_dout
    gj = 0
    hs = 0

    for kh in range(NKV):
        bk = kh % 2
        wk_ = [kv_done] + ([kbuf_free[bk]] if kbuf_free[bk] else [])
        t_kl = dma("sync", kbuf[bk][:], k_sc[kh * P:(kh + 1) * P, :], waits=wk_,
                   q="dkv%d" % bk)
        wv_ = [kv_done] + ([vbuf_free[bk]] if vbuf_free[bk] else [])
        t_vl = dma("sync", vbuf[bk][:].rearrange("p (kb c) -> p kb c", kb=32),
                   v3[:, :, kh * HD:(kh + 1) * HD], waits=wv_, q="dkv%d" % bk)
        for hh in range(4):
            h = kh * 4 + hh
            bq = h % 2
            wq_ = [b_start_dout]
            if qbuf_free[bq]:
                wq_.append(qbuf_free[bq])
            t_ql = dma("sync", qbuf[bq][:], q_sc[h * P:(h + 1) * P, :], waits=wq_,
                       q="dq%d" % bq)
            for s in range(4):
                cnt = SLOT_CNT[s]
                hs_cur = hs
                pv_slot = psPVv[hs_cur % 2]
                G = cnt // 2
                sc_tgt = [None] * G
                sc_slot = [None] * G

                def plan_scores(j, bk=bk, bq=bq, s=s):
                    bs = (gj0 + j) % 2
                    kb0, kb1 = 2 * j, 2 * j + 1
                    wt = [t_kl, t_vl, t_ql]
                    if psS_free[bs]:
                        wt.append(psS_free[bs])
                    if hs == 0 and j == 0:
                        wt.append(("v", t_mc[1]))
                    def smm_fn(e, bs=bs, kb0=kb0, kb1=kb1, bk=bk, bq=bq, s=s):
                        qv_ = qbuf[bq][:, s * 512:(s + 1) * 512]
                        e.matmul(psSv[bs][:, 0:512], kbuf[bk][:, kb0 * 128:(kb0 + 1) * 128],
                                 qv_, start=True, stop=True)
                        return e.matmul(psSv[bs][:, 512:1024], kbuf[bk][:, kb1 * 128:(kb1 + 1) * 128],
                                        qv_, start=True, stop=True)
                    sc_tgt[j] = A("tensor", smm_fn, waits=wt, inc="t")
                    sc_slot[j] = bs

                gj0 = gj
                plan_scores(0)
                t_pv_last = None
                t_ls_last = None
                pending_pv = [None]
                def drain_pv():
                    if pending_pv[0] is not None:
                        pending_pv[0]()
                        pending_pv[0] = None
                for j in range(G):
                    bs = sc_slot[j]
                    bp = (gj0 + j) % 4
                    kb0, kb1 = 2 * j, 2 * j + 1
                    if j + 1 < G:
                        plan_scores(j + 1)
                    drain_pv()
                    we = [("t", sc_tgt[j][1])] + ([pT_free[bp]] if pT_free[bp] else [])
                    def exp_fn(e, bp=bp, bs=bs, h=h):
                        return e.activation(pT[bp][:], psSv[bs], AF.Exp,
                                            bias=m_bias[:, h:h + 1], scale=INV_SQRT_HD)
                    t_ex = A("scalar", exp_fn, waits=we, inc="a")
                    psS_free[bs] = ("a", t_ex[1])
                    dep = ("a", t_ex[1])
                    if 2 * j >= cnt - 8:
                        w0 = kb0 - (cnt - 8)
                        w1 = kb1 - (cnt - 8)
                        wm = [dep, t_msk]
                        def msk_fn(e, bp=bp, s=s, w0=w0, w1=w1):
                            m0 = masks_sb[:, (s * 8 + w0) * 512:(s * 8 + w0 + 1) * 512]
                            m1 = masks_sb[:, (s * 8 + w1) * 512:(s * 8 + w1 + 1) * 512]
                            e.tensor_tensor(pT[bp][:, 0:512], pT[bp][:, 0:512], m0, ALU.mult)
                            return e.tensor_tensor(pT[bp][:, 512:1024], pT[bp][:, 512:1024], m1, ALU.mult)
                        t_mk = A("vector", msk_fn, waits=wm, inc="v")
                        dep = ("v", t_mk[1])
                    def mk_pvls(dep=dep, bp=bp, kb0=kb0, kb1=kb1, j=j, cnt=cnt, bk=bk):
                        def emit_pvls():
                            nonlocal t_pv_last, t_ls_last
                            wpv = [dep]
                            if j == 0 and psPV_free[hs_cur % 2]:
                                wpv.append(psPV_free[hs_cur % 2])
                            def pv_fn(e, bp=bp, kb0=kb0, kb1=kb1, bk=bk, slot=pv_slot, j=j, cnt=cnt):
                                e.matmul(slot, vbuf[bk][:, kb0 * 128:(kb0 + 1) * 128],
                                         pT[bp][:, 0:512], start=(j == 0), stop=False)
                                return e.matmul(slot, vbuf[bk][:, kb1 * 128:(kb1 + 1) * 128],
                                                pT[bp][:, 512:1024], start=False,
                                                stop=(2 * j + 1 == cnt - 1))
                            t_pv_last = A("tensor", pv_fn, waits=wpv, inc="t")
                            wls = [dep]
                            if j == 0 and lsum_free2[hs_cur % 2]:
                                wls.append(lsum_free2[hs_cur % 2])
                            def ls_fn(e, bp=bp, j=j, cnt=cnt, pls=psLSv[hs_cur % 2]):
                                e.matmul(pls, ones_b[:, 0:1], pT[bp][:, 0:512],
                                         start=(j == 0), stop=False)
                                return e.matmul(pls, ones_b[:, 0:1], pT[bp][:, 512:1024],
                                                start=False, stop=(2 * j + 1 == cnt - 1))
                            t_ls_last = A("tensor", ls_fn, waits=wls, inc="t")
                            pT_free[bp] = ("t", t_ls_last[1])
                        return emit_pvls
                    pending_pv[0] = mk_pvls()
                    gj += 1
                drain_pv()
                def rec_fn(e, pls=psLSv[hs_cur % 2]):
                    e.reciprocal(strip[0:1, 1024:1536], pls)
                    return e.tensor_copy(stripb[0:1, 512:1024], strip[0:1, 1024:1536])
                t_rc = A("vector", rec_fn, waits=[("t", t_ls_last[1])], inc="v")
                wlb = [("v", t_rc[1])] + ([lbc_free] if lbc_free else [])
                t_lb = A("tensor", lambda e, plb=psLBv[hs_cur % 2]: e.matmul(
                             plb[:, 0:512], ones_b[0:1, :],
                             stripb[0:1, 512:1024], start=True, stop=True),
                         waits=wlb, inc="t")
                ba = hs_cur % 2
                wsc = [("t", t_lb[1]), ("t", t_pv_last[1])]
                if aout_free[ba]:
                    wsc.append(dnow())
                def scale_fn(e, ba=ba, slot=pv_slot, plb=psLBv[hs_cur % 2]):
                    e.tensor_copy(linv_bc[:], plb[:, 0:512])
                    return e.tensor_tensor(aout[ba][:], slot, linv_bc[:], ALU.mult)
                t_sc2 = A("vector", scale_fn, waits=wsc, inc="v")
                lbc_free = ("v", t_sc2[1])
                lsum_free2[hs_cur % 2] = ("v", t_sc2[1])
                psPV_free[hs_cur % 2] = ("v", t_sc2[1])
                t_as = dma("scalar", attn_sc[h * P:(h + 1) * P, s * 512:(s + 1) * 512],
                           aout[ba][:], waits=[("v", t_sc2[1])], q="dout")
                aout_free[ba] = ("dout", t_as[1])
                attn_written[(h, s)] = t_as[1]
                hs += 1
            qbuf_free[bq] = ("t", pl.cnt["t"])
        kbuf_free[bk] = ("t", pl.cnt["t"])
        vbuf_free[bk] = ("t", pl.cnt["t"])

    pl.barrier()

    c2_start_dout = dnow()
    wstgC_free = [None, None]
    ybuf2_free = [None, None]
    ysq2_free = [None, None]
    fbuf_free = [None, None]
    wbfC_free = [None, None]
    psO_free = [None, None]
    xt_free = [None, None]
    obuf_free = [None, None]
    psOv = [psA[:, 0:1024], psA[:, 1024:2048]]
    oidx = [0]
    for th in range(2):
        adin = []
        for dc in range(DC):
            w = []
            if th == 0 and dc == 0:
                w.append(c2_start_dout)
            if th == 1 and dc == 0:
                w.append(("t", pl.cnt["t"]))
            t_a = dma("sync", abuf[:, dc * 1024:(dc + 1) * 1024],
                      attn_sc[dc * P:(dc + 1) * P, th * 1024:(th + 1) * 1024], waits=w,
                      q="dab")
            adin.append(t_a)
        last_sq = None
        for oc in range(DC):
            by = oc % 2
            wy = [ybuf2_free[by]] if ybuf2_free[by] else []
            if th == 0 and oc == 0:
                wy.append(c2_start_dout)
            if th == 1 and oc == 0:
                wy.append(("v", pl.cnt["v"]))
            t_y = dma("sync", ybuf2[by][:],
                      y_sc[oc * P:(oc + 1) * P, th * 1024:(th + 1) * 1024], waits=wy,
                      q="dy%d" % by)
            wv_ = [t_y] + ([ysq2_free[by]] if ysq2_free[by] else [])
            t_sq2 = A("vector", lambda e, by=by: e.tensor_mul(ysq2[by][:], ybuf2[by][:],
                                                              ybuf2[by][:]), waits=wv_, inc="v")
            def ymm_fn(e, oc=oc, by=by):
                st, sp = oc == 0, oc == DC - 1
                e.matmul(psB[0:1, 0:512], ones_b[:, 0:1], ybuf2[by][:, 0:512], start=st, stop=sp)
                e.matmul(psB[0:1, 512:1024], ones_b[:, 0:1], ybuf2[by][:, 512:1024], start=st, stop=sp)
                e.matmul(psB[0:1, 1024:1536], ones_b[:, 0:1], ysq2[by][:, 0:512], start=st, stop=sp)
                return e.matmul(psB[0:1, 1536:2048], ones_b[:, 0:1], ysq2[by][:, 512:1024],
                                start=st, stop=sp)
            t_my = A("tensor", ymm_fn, waits=[("v", t_sq2[1])], inc="t")
            ybuf2_free[by] = ("t", t_my[1])
            ysq2_free[by] = ("t", t_my[1])
            last_sq = t_my
        def stripC_fn(e):
            e.tensor_scalar_mul(strip[0:1, 0:1024], psB[0:1, 0:1024], 1.0 / D)
            e.tensor_scalar_mul(strip[0:1, 1024:2048], psB[0:1, 1024:2048], 1.0 / D)
            e.tensor_mul(otmp[0:1, 0:1024], strip[0:1, 0:1024], strip[0:1, 0:1024])
            e.tensor_sub(strip[0:1, 1024:2048], strip[0:1, 1024:2048], otmp[0:1, 0:1024])
            return e.tensor_scalar_add(strip[0:1, 1024:2048], strip[0:1, 1024:2048], EPS)
        t_sp = A("vector", stripC_fn, waits=[("t", last_sq[1])], inc="v")
        t_ac = A("scalar", lambda e: e.activation(otmp[0:1, 0:1024], strip[0:1, 1024:2048],
                                                  AF.Sqrt), waits=[("v", t_sp[1])], inc="a")
        def strip2C_fn(e):
            e.reciprocal(strip[0:1, 1024:2048], otmp[0:1, 0:1024])
            e.tensor_mul(strip[0:1, 0:1024], strip[0:1, 0:1024], strip[0:1, 1024:2048])
            e.tensor_copy(stripb[0:1, 0:1024], strip[0:1, 1024:2048])
            return e.tensor_copy(ybuf2[0][0:1, 0:1024], strip[0:1, 0:1024])
        t_s2 = A("vector", strip2C_fn, waits=[("a", t_ac[1])], inc="v")
        def bcC_fn(e):
            e.matmul(psB[:, 0:512], ones_b[0:1, :], stripb[0:1, 0:512], start=True, stop=True)
            e.matmul(psB[:, 512:1024], ones_b[0:1, :], stripb[0:1, 512:1024], start=True, stop=True)
            e.matmul(psB[:, 1024:1536], ones_b[0:1, :], ybuf2[0][0:1, 0:512], start=True, stop=True)
            return e.matmul(psB[:, 1536:2048], ones_b[0:1, :], ybuf2[0][0:1, 512:1024],
                            start=True, stop=True)
        t_bc = A("tensor", bcC_fn, waits=[("v", t_s2[1])], inc="t")
        ybuf2_free[0] = ("t", t_bc[1])
        bc_ready = ("t", t_bc[1])
        for w2 in range(16):
            bsel = w2 % 2
            cast_tgts = [None, None]
            for half in range(2):
                t_w = dma("sync", wstgC[half][:].rearrange("p (c f) -> p c f", c=DC),
                          wo3[:, :, w2 * 256 + half * 128: w2 * 256 + (half + 1) * 128],
                          waits=[wstgC_free[half]], q="dw3%d" % half)
                wc = [t_w] + ([wbfC_free[bsel]] if (wbfC_free[bsel] and half == 0) else [])
                dst3 = bass.AP(wbfC[bsel], wbfC[bsel][:].offset + half * 128,
                               [list(wbfC[bsel][:].ap[0]), [256, DC], [1, 128]])
                def castC_fn(e, half=half, dst3=dst3):
                    return e.tensor_copy(dst3, wstgC[half][:].rearrange("p (c f) -> p c f", c=DC))
                t_cast = A("vector", castC_fn, waits=wc, inc="v")
                wstgC_free[half] = t_cast
                cast_tgts[half] = t_cast
            for sub in range(2):
                oc = w2 * 2 + sub
                bo = oidx[0] % 2
                b2o = oidx[0] % 2
                oidx[0] += 1
                t_cast = cast_tgts[sub]
                wt = [("v", t_cast[1]), adin[-1]]
                if psO_free[bo]:
                    wt.append(psO_free[bo])
                def omm_fn(e, bsel=bsel, sub=sub, bo=bo):
                    for dc in range(DC):
                        e.matmul(psOv[bo][:, 0:512],
                                 wbfC[bsel][:, dc * 256 + sub * 128: dc * 256 + sub * 128 + 128],
                                 abuf[:, dc * 1024: dc * 1024 + 512],
                                 start=(dc == 0), stop=(dc == DC - 1))
                    for dc in range(DC):
                        r = e.matmul(psOv[bo][:, 512:1024],
                                     wbfC[bsel][:, dc * 256 + sub * 128: dc * 256 + sub * 128 + 128],
                                     abuf[:, dc * 1024 + 512: dc * 1024 + 1024],
                                     start=(dc == 0), stop=(dc == DC - 1))
                    return r
                t_mm = A("tensor", omm_fn, waits=wt, inc="t")
                if sub == 1:
                    wbfC_free[bsel] = ("t", t_mm[1])
                t_xl = dma("sync", xtile[b2o][:],
                           xown[oc * P:(oc + 1) * P, th * 1024:(th + 1) * 1024],
                           waits=[xt_free[b2o]], q="dxf%d" % b2o)
                by = oc % 2
                wy = [bc_ready] + ([ybuf2_free[by]] if ybuf2_free[by] else [])
                t_y2 = dma("sync", ybuf2[by][:],
                           y_sc[oc * P:(oc + 1) * P, th * 1024:(th + 1) * 1024], waits=wy,
                           q="dy%d" % by)
                wn = [t_y2] + ([fbuf_free[by]] if fbuf_free[by] else [])
                def norm_fn(e, oc=oc, by=by):
                    e.tensor_tensor(fbuf[by][:], ybuf2[by][:], psB[:, 0:1024], ALU.mult)
                    e.tensor_tensor(fbuf[by][:], fbuf[by][:], psB[:, 1024:2048], ALU.subtract)
                    return e.tensor_scalar(fbuf[by][:], fbuf[by][:], gam_col[:, oc:oc + 1],
                                           bet_col[:, oc:oc + 1], ALU.mult, ALU.add)
                t_nm = A("vector", norm_fn, waits=wn, inc="v")
                ybuf2_free[by] = ("v", t_nm[1])
                wv_ = [("t", t_mm[1]), t_xl, ("v", t_nm[1])]
                if obuf_free[b2o]:
                    wv_.append(dnow())
                def oadd_fn(e, bo=bo, b2o=b2o, by=by):
                    e.tensor_tensor(otmp[:], psOv[bo], xtile[b2o][:], ALU.add)
                    return e.tensor_tensor(obuf[b2o][:], otmp[:], fbuf[by][:], ALU.add)
                t_oa = A("vector", oadd_fn, waits=wv_, inc="v")
                fbuf_free[by] = ("v", t_oa[1])
                psO_free[bo] = ("v", t_oa[1])
                xt_free[b2o] = ("v", t_oa[1])
                t_os = dma("scalar", outf[oc * P:(oc + 1) * P, th * 1024:(th + 1) * 1024],
                           obuf[b2o][:], waits=[("v", t_oa[1])], q="dout")
                obuf_free[b2o] = ("dout", t_os[1])

    if dbg:
        pl.barrier()
        for nm, src, shp, dt in [
            ("d_xn", xn_sc, [D, 4096], BF16),
            ("d_k", k_sc, [NKV * P, 4096], BF16),
            ("d_v", v_sc, [4096, NKV * HD], BF16),
            ("d_q", q_sc, [32 * P, TOWN], BF16),
            ("d_y", y_sc, [D, TOWN], BF16),
            ("d_ffn", ffn_sc, [D, TOWN], F32),
            ("d_attn", attn_sc, [32 * P, TOWN], BF16),
        ]:
            dst = nc.dram_tensor(nm, shp, dt, kind="ExternalOutput").ap()
            dma("sync", dst, src, q="dc")
        d_st = nc.dram_tensor("d_st", [P, 72], F32, kind="ExternalOutput").ap()
        stt = sb("d_stt", [P, 72], F32)
        A("vector", lambda e: (e.tensor_copy(stt[:, 0:8], ksqmax[:]),
                               e.tensor_copy(stt[:, 8:40], qsqmax[:]),
                               e.tensor_copy(stt[:, 40:72], m_bias[:]))[-1], inc="v")
        dma("sync", d_st, stt[:], waits=[("v", pl.cnt["v"])], q="dc")
    final = pl.snap()

    import contextlib
    with contextlib.ExitStack() as stack:
        sems = {}
        for name in pl.cnt:
            sems[name] = stack.enter_context(nc.semaphore("s_" + name))
        block = stack.enter_context(nc.Block())
        own_of = {"tensor": "t", "vector": "v", "scalar": "a", "gpsimd": "p",
                  "sync": None}
        dma_sems = {n for n in pl.cnt if n not in ("t", "v", "a", "p")}

        def emit(eng_name):
            def go(eng):
                own = own_of[eng_name]
                for fn, waits, tgt in pl.ops[eng_name]:
                    for s, val in waits:
                        if s == own:
                            continue
                        eng.wait_ge(sems[s], val)
                    inst = fn(eng) if fn is not None else None
                    if tgt is not None:
                        s, val = tgt
                        if inst is None:
                            inst = eng.nop()
                        inst.then_inc(sems[s], 16 if s in dma_sems else 1)
                if eng_name == "sync":
                    eng.wait_ge(sems["dout"], final["dout"])
            return go

        block.tensor(emit("tensor"))
        block.vector(emit("vector"))
        block.scalar(emit("scalar"))
        block.gpsimd(emit("gpsimd"))
        block.sync(emit("sync"))

    return nc


def _host_tables(j):
    inv = 10000.0 ** (-(np.arange(64, dtype=np.float64) * 2.0 / 128.0))
    pos = np.arange(4096, dtype=np.float64)
    fr = pos[None, :] * inv[:, None]
    cos = np.cos(fr).astype(np.float32)
    sin = np.sin(fr).astype(np.float32)
    cosb = np.concatenate([cos, cos], 0)
    sinx = np.concatenate([-sin, sin], 0)
    own_cols = np.concatenate([np.arange(b * 512, (b + 1) * 512) for b in OWNBLK[j]])
    cosq = np.ascontiguousarray(cosb[:, own_cols])
    sinq = np.ascontiguousarray(sinx[:, own_cols])
    m = np.zeros((32, 128, 512), np.float32)
    for s in range(4):
        ob = OWNBLK[j][s]
        cnt = SLOT_CNT[s]
        qabs = ob * 512 + np.arange(512)
        for w in range(8):
            kb = cnt - 8 + w
            kabs = kb * 128 + np.arange(128)
            m[s * 8 + w] = (kabs[:, None] <= qabs[None, :]).astype(np.float32)
    bf = ml_dtypes.bfloat16
    return (cosb.astype(bf), sinx.astype(bf), cosq.astype(bf), sinq.astype(bf),
            np.ascontiguousarray(m.reshape(32 * 128, 512)).astype(bf), own_cols)


def kernel(L0_attn_x, L0_attn_Wq, L0_attn_Wk, L0_attn_Wv, L0_attn_Wo,
           L0_ffn_weight, L0_ffn_bias, L0_ffn_g, L0_ffn_h,
           L0_ffn_ln_gamma, L0_ffn_ln_beta, _dbg=False):
    x = np.ascontiguousarray(np.asarray(L0_attn_x, dtype=np.float32))
    wq_ = np.ascontiguousarray(np.asarray(L0_attn_Wq, dtype=np.float32))
    wk_ = np.ascontiguousarray(np.asarray(L0_attn_Wk, dtype=np.float32))
    wv_ = np.ascontiguousarray(np.asarray(L0_attn_Wv, dtype=np.float32))
    wo_ = np.ascontiguousarray(np.asarray(L0_attn_Wo, dtype=np.float32))
    wfT_ = np.ascontiguousarray(np.asarray(L0_ffn_weight, dtype=np.float32).T)

    def col(v):
        return np.ascontiguousarray(np.asarray(v, dtype=np.float32).reshape(DC, P).T)

    vecs_ = np.ascontiguousarray(np.concatenate(
        [col(L0_ffn_bias), col(L0_ffn_g), col(L0_ffn_h),
         col(L0_ffn_ln_gamma), col(L0_ffn_ln_beta),
         np.zeros((P, DC), np.float32)], axis=1))

    tabs = [_host_tables(0), _host_tables(1)]
    xT = [np.ascontiguousarray(x[b].T) for b in range(4)]

    nc = build_nc(dbg=_dbg)
    in_maps = []
    for c in range(8):
        b, j = c // 2, c % 2
        cosb_, sinx_, cosq_, sinq_, masks_, own_cols = tabs[j]
        xown_ = np.ascontiguousarray(xT[b][:, own_cols])
        in_maps.append({
            "xT": xT[b], "xown": xown_, "wq": wq_, "wk": wk_, "wv": wv_,
            "wo": wo_, "wfT": wfT_, "vecs": vecs_,
            "cosb": cosb_, "sinx": sinx_, "cosq": cosq_, "sinq": sinq_,
            "masks": masks_,
        })
    res = run_bass_kernel_spmd(nc, in_maps, core_ids=list(range(8)))
    out = np.empty((4, 4096, 4096), np.float32)
    for c in range(8):
        b, j = c // 2, c % 2
        of = np.asarray(res.results[c]["outf"])
        for s in range(4):
            ob = OWNBLK[j][s]
            out[b, ob * 512:(ob + 1) * 512, :] = of[:, s * 512:(s + 1) * 512].T
    if _dbg:
        return out, res
    return out


if __name__ == "__main__":
    import reference as _r
    inp = _r.setup_inputs()
    o = kernel(**{k: np.asarray(v) for k, v in inp.items()})
    e = np.asarray(_r.reference(**inp))
    err = o - e
    print("rvr", float((err ** 2).mean() / (np.asarray(e) ** 2).mean()),
          "max", float(np.abs(err).max()))
